# version 11
# speedup vs baseline: 1.6692x; 1.3686x over previous
import contextlib
import math

import numpy as np
import ml_dtypes

import concourse.bass as bass
import concourse.mybir as mybir
from concourse.bass_utils import run_bass_kernel_spmd

F32 = mybir.dt.float32
BF16 = mybir.dt.bfloat16
I32 = mybir.dt.int32
ALU = mybir.AluOpType
AF = mybir.ActivationFunctionType
AX = mybir.AxisListType
BF = ml_dtypes.bfloat16

D = 4096
T = 2048
B = 4
TPC = 1024
NTT = TPC // 128
NH, NKV, HD = 32, 8, 128
KD = NKV * HD
NDC = D // 128
NE, RK = 8, 16
ER = NE * RK
EPS = 1e-5
NEGBIG = -1e30
ISQ = 1.0 / math.sqrt(HD)


class Sem:

    def __init__(self, h):
        self.h = h
        self.n = 0

    def inc(self, instr, k=1):
        instr.then_inc(self.h, k)
        self.n += k
        return self.n


class Ctx:
    def __init__(self):
        self.nc = bass.Bass("TRN2", target_bir_lowering=False,
                            detect_race_conditions=False)
        self.es = contextlib.ExitStack()
        self.sems = []

    def sb(self, name, shape, dt):
        return self.es.enter_context(self.nc.sbuf_tensor(name, shape, dt))

    def ps(self, name, shape, dt=F32):
        return self.es.enter_context(self.nc.psum_tensor(name, shape, dt))

    def sem(self, name):
        s = Sem(self.es.enter_context(self.nc.semaphore(name)))
        self.sems.append(s)
        return s

    def begin(self):
        self.block = self.es.enter_context(self.nc.Block())

    def finish(self):
        self.es.close()
        return self.nc


class Sub:

    def __init__(self, nc):
        self.nc = nc
        self.es = contextlib.ExitStack()

    def sb(self, name, shape, dt):
        return self.es.enter_context(self.nc.sbuf_tensor(name, shape, dt))

    def ps(self, name, shape, dt=F32):
        return self.es.enter_context(self.nc.psum_tensor(name, shape, dt))

    def close(self):
        self.es.close()


def fence(nc, sems, engines):
    for e in engines:
        for s in sems:
            if s.n:
                e.wait_ge(s.h, s.n)


class QSem:

    def __init__(self, c, name, nslots=2):
        self.sems = [c.sem(f"{name}_{i}") for i in range(nslots)]
        self.nslots = nslots

    def dma(self, eng, slot, out, in_):
        s = self.sems[slot % self.nslots]
        if s.n:
            eng.wait_ge(s.h, s.n)
        s.inc(eng.dma_start(out=out, in_=in_), 16)
        return (slot % self.nslots, s.n)

    def idma(self, eng, slot, out, out_off, in_):
        s = self.sems[slot % self.nslots]
        if s.n:
            eng.wait_ge(s.h, s.n)
        s.inc(eng.indirect_dma_start(out, out_off, in_, None), 16)
        return (slot % self.nslots, s.n)

    def wait(self, eng, tok):
        slot, val = tok
        eng.wait_ge(self.sems[slot].h, val)

    def wait_all(self, eng):
        for s in self.sems:
            if s.n:
                eng.wait_ge(s.h, s.n)

    @property
    def all(self):
        return self.sems


class Consts:

    def __init__(self, c, ev, n_max=8):
        self.buf = c.sb("consts", [128, n_max], F32)
        self.vals = {}
        self.ev = ev
        self.ready = None

    def ap(self, nc, value):
        if value not in self.vals:
            assert self.ready is None, "consts sealed"
            idx = len(self.vals)
            self.vals[value] = idx
            nc.vector.memset(self.buf[:, idx:idx + 1], float(value))
        i = self.vals[value]
        return self.buf[:, i:i + 1]

    def seal(self, nc):
        idx = len(self.vals)
        self.ready = self.ev.inc(nc.vector.memset(self.buf[:, idx:idx + 1], 0.0))
        return self.ready


def emit_ln_stats(nc, x_ap, scr, scr2, mean, mean2, rstd, nmb, ev, ac, eps_ap, width=D):
    v, s = nc.vector, nc.scalar
    inv = 1.0 / width
    sq = math.sqrt(inv)
    n1 = ev.inc(v.tensor_scalar(scr, x_ap, inv, 0.0, ALU.mult, ALU.add,
                                accum_out=mean))
    n2 = ac.inc(s.activation(scr2, x_ap, AF.Square, bias=0.0, scale=sq,
                             accum_out=mean2))
    v.wait_ge(ac.h, n2)
    v.wait_ge(ev.h, n1)
    n3 = ev.inc(v.tensor_tensor(rstd, mean, mean, ALU.mult))
    v.wait_ge(ev.h, n3)
    n4 = ev.inc(v.tensor_tensor(rstd, mean2, rstd, ALU.subtract))
    s.wait_ge(ev.h, n4)
    n5 = ac.inc(s.activation(rstd, rstd, AF.Sqrt, bias=eps_ap, scale=1.0))
    v.wait_ge(ac.h, n5)
    n6 = ev.inc(v.reciprocal(rstd, rstd))
    v.wait_ge(ev.h, n6)
    n7 = ev.inc(v.tensor_scalar(nmb, mean, -1.0, 0.0, ALU.mult, ALU.add))
    v.wait_ge(ev.h, n7)
    n8 = ev.inc(v.tensor_tensor(nmb, nmb, rstd, ALU.mult))
    return n8


def emit_ln_transpose_phase(nc, c, xin_ap, x2T, ev, ac, mm, xq, ident, eps_ap):
    g, v, s, te = nc.gpsimd, nc.vector, nc.scalar, nc.tensor
    xt = [c.sb("ln_xt0", [128, D], F32), c.sb("ln_xt1", [128, D], F32)]
    scr = c.sb("ln_scr", [128, D], BF16)
    scr2 = c.sb("ln_scr2", [128, D], BF16)
    stat = c.sb("ln_stat", [128, 8], F32)
    x2 = [c.sb("ln_x2a", [128, D], BF16), c.sb("ln_x2b", [128, D], BF16)]
    ptr = [c.ps("ln_ptr0", [128, 1024], BF16), c.ps("ln_ptr1", [128, 1024], BF16)]

    ld_tile = []
    ac_norm = {}
    ev_copy = {}
    mm_tr = {}

    x2T3 = x2T[:, :].rearrange("p (c t) -> p c t", c=NDC)

    for i in range(NTT):
        if i >= 2:
            g.wait_ge(ac.h, ac_norm[i - 2])
        ld_tile.append(xq.dma(g, i, xt[i % 2][:, :],
                              xin_ap[i * 128:(i + 1) * 128, :]))

        sl = (i % 2) * 4
        mean, mean2 = stat[:, sl:sl + 1], stat[:, sl + 1:sl + 2]
        rstd, nmb = stat[:, sl + 2:sl + 3], stat[:, sl + 3:sl + 4]
        xq.wait(v, ld_tile[i])
        xq.wait(s, ld_tile[i])
        evn = emit_ln_stats(nc, xt[i % 2][:, :], scr[:, :], scr2[:, :], mean, mean2, rstd,
                            nmb, ev, ac, eps_ap)
        s.wait_ge(ev.h, evn)
        if i >= 2:
            s.wait_ge(mm.h, mm_tr[(i - 2, 3)])
        ac_norm[i] = ac.inc(s.activation(x2[i % 2][:, :], xt[i % 2][:, :],
                                         AF.Identity, bias=nmb, scale=rstd))
        te.wait_ge(ac.h, ac_norm[i])
        for gr in range(4):
            bank = ptr[gr % 2]
            if (i, gr) not in ((0, 0), (0, 1)):
                prev = (i, gr - 2) if gr >= 2 else (i - 1, gr + 2)
                te.wait_ge(ev.h, ev_copy[prev])
            for k in range(8):
                dc = gr * 8 + k
                ins = te.transpose(bank[:, k * 128:(k + 1) * 128],
                                   x2[i % 2][:, dc * 128:(dc + 1) * 128],
                                   ident[:, :])
            mm_tr[(i, gr)] = mm.inc(ins)
            v.wait_ge(mm.h, mm_tr[(i, gr)])
            dst = x2T3[:, gr * 8:(gr + 1) * 8, i * 128:(i + 1) * 128]
            src = bank[:, :].rearrange("p (c t) -> p c t", c=8)
            ev_copy[(i, gr)] = ev.inc(v.tensor_copy(dst, src))
    ev_all = ev.n
    return ev_all


def emit_modeF_proj(nc, c, x2T, w_dram, n_oc, wsl, och, pmm, wq, mm, ev, stq,
                    x2T_ready_ev, rope=None, out_dram=None, g_dma=None):
    g, v, s, te = nc.gpsimd, nc.vector, nc.scalar, nc.tensor
    wt_sl = {}
    ev_ch = {}
    st_ch = {}
    mm_ch = {}
    for oc in range(n_oc):
        if oc >= 2:
            g.wait_ge(mm.h, mm_ch[oc - 2])
        wt_sl[oc] = wq.dma(g, oc, wsl[oc % 2][:, :], w_dram[oc])
        wq.wait(te, wt_sl[oc])
        if oc == 0:
            te.wait_ge(ev.h, x2T_ready_ev)
        for tg in range(2):
            bank = pmm[tg]
            if oc >= 1:
                te.wait_ge(ev.h, ev_ch[(oc - 1, tg)])
            for dc in range(NDC):
                ins = te.matmul(bank[:, :], wsl[oc % 2][:, dc * 128:(dc + 1) * 128],
                                x2T[:, dc * TPC + tg * 512: dc * TPC + tg * 512 + 512],
                                start=(dc == 0), stop=(dc == NDC - 1))
            mm_ch[oc] = mm.inc(ins)
            v.wait_ge(mm.h, mm_ch[oc])
            if oc >= 2:
                stq.wait(v, st_ch[oc - 2])
            ev_ch[(oc, tg)] = ev.inc(v.tensor_copy(
                och[oc % 2][:, tg * 512:(tg + 1) * 512], bank[:, :]))
        if rope is not None:
            cos_ap, sin_ap, rot, rt1 = rope
            q = och[oc % 2][:, :]
            v.wait_ge(ev.h, ev_ch[(oc, 1)])
            n0 = ev.inc(v.tensor_copy(rot[0:64, :], q[64:128, :]))
            n0 = ev.inc(v.tensor_copy(rot[64:128, :], q[0:64, :]))
            v.wait_ge(ev.h, n0)
            n0 = ev.inc(v.tensor_tensor(rt1[:, :], rot[:, :], sin_ap, ALU.mult))
            v.wait_ge(ev.h, n0)
            n0 = ev.inc(v.tensor_tensor(rot[:, :], q, cos_ap, ALU.mult))
            v.wait_ge(ev.h, n0)
            n0 = ev.inc(v.tensor_tensor(q, rot[:, :], rt1[:, :], ALU.add))
            ev_ch[(oc, "rope")] = n0
        gate = ev_ch.get((oc, "rope"), ev_ch[(oc, 1)])
        dma_eng = g_dma if g_dma is not None else nc.sync
        dma_eng.wait_ge(ev.h, gate)
        st_ch[oc] = stq.dma(dma_eng, oc, out_dram[oc * 128:(oc + 1) * 128, :],
                            och[oc % 2][:, :])
    return st_ch[n_oc - 1]


def build_qkv():
    c = Ctx()
    nc = c.nc
    xin = nc.dram_tensor("xin", [TPC, D], F32, kind="ExternalInput").ap()
    wq_r = nc.dram_tensor("wq_r", [NDC, 128, D], F32, kind="ExternalInput").ap()
    wk_r = nc.dram_tensor("wk_r", [KD // 128, 128, D], F32, kind="ExternalInput").ap()
    wv_r = nc.dram_tensor("wv_r", [KD // 512, 128, NDC * 512], F32, kind="ExternalInput").ap()
    qcos = nc.dram_tensor("qcos", [128, TPC], F32, kind="ExternalInput").ap()
    qsin = nc.dram_tensor("qsin", [128, TPC], F32, kind="ExternalInput").ap()
    kcos = nc.dram_tensor("kcos", [128, TPC], F32, kind="ExternalInput").ap()
    ksin = nc.dram_tensor("ksin", [128, TPC], F32, kind="ExternalInput").ap()
    qt = nc.dram_tensor("qt", [D, TPC], F32, kind="ExternalOutput").ap()
    kt = nc.dram_tensor("kt", [KD, TPC], F32, kind="ExternalOutput").ap()
    vt = nc.dram_tensor("vt", [TPC, KD], F32, kind="ExternalOutput").ap()
    ident_d = nc.inline_tensor(np.eye(128, dtype=BF), "ident_c").ap()

    x2T = c.sb("x2T", [128, NDC * TPC], BF16)
    ident = c.sb("ident", [128, 128], BF16)
    tabs = c.sb("tabs", [128, 4 * TPC], F32)
    wsl = [c.sb(f"wsl{i}", [128, D], BF16) for i in range(2)]
    wvsl = c.sb("wvsl", [128, NDC * 512], BF16)
    och = [c.sb(f"och{i}", [128, TPC], BF16) for i in range(2)]
    rot = c.sb("rot", [128, TPC], BF16)
    rt1 = c.sb("rt1", [128, TPC], BF16)
    vsm = [c.sb(f"vsm{i}", [128, 512], F32) for i in range(2)]
    pmm = [c.ps(f"pmm{i}", [128, 512], F32) for i in range(2)]

    mm = c.sem("mm"); ev = c.sem("ev"); ac = c.sem("ac")
    lds = c.sem("lds")
    xq = QSem(c, "xld", 2)
    wq2 = QSem(c, "wsq", 2)
    stq = QSem(c, "soq", 2)
    wvq = QSem(c, "wvq", 1)
    vst = QSem(c, "vst", 2)
    consts = Consts(c, None)
    c.begin()
    g, v, s, te, sy = nc.gpsimd, nc.vector, nc.scalar, nc.tensor, nc.sync
    consts.ev = ev
    eps_ap = consts.ap(nc, EPS)
    cready = consts.seal(nc)
    s.wait_ge(ev.h, cready)

    lds.inc(sy.dma_start(out=ident[:, :], in_=ident_d), 16)
    lds.inc(sy.dma_start(out=tabs[:, 0:TPC], in_=qcos), 16)
    lds.inc(sy.dma_start(out=tabs[:, TPC:2 * TPC], in_=qsin), 16)
    lds.inc(sy.dma_start(out=tabs[:, 2 * TPC:3 * TPC], in_=kcos), 16)
    lds.inc(sy.dma_start(out=tabs[:, 3 * TPC:4 * TPC], in_=ksin), 16)
    ident_ld = lds.n
    te.wait_ge(lds.h, ident_ld)
    v.wait_ge(lds.h, ident_ld)

    x2T_ev = emit_ln_transpose_phase(nc, c, xin, x2T, ev, ac, mm, xq, ident, eps_ap)
    alleng = [g, v, s, te, sy]
    fence(nc, c.sems, alleng)

    emit_modeF_proj(nc, c, x2T, wq_r, NDC, wsl, och, pmm, wq2, mm, ev, stq,
                    x2T_ev, rope=(tabs[:, 0:TPC], tabs[:, TPC:2 * TPC], rot, rt1),
                    out_dram=qt, g_dma=g)
    fence(nc, c.sems, alleng)
    emit_modeF_proj(nc, c, x2T, wk_r, KD // 128, wsl, och, pmm, wq2, mm, ev, stq,
                    x2T_ev, rope=(tabs[:, 2 * TPC:3 * TPC], tabs[:, 3 * TPC:4 * TPC], rot, rt1),
                    out_dram=kt, g_dma=g)
    fence(nc, c.sems, alleng)

    vt_st = {}
    vt_ev = {}
    for oc in range(KD // 512):
        if oc >= 1:
            g.wait_ge(mm.h, mm.n)
        wtn = wvq.dma(g, 0, wvsl[:, :], wv_r[oc])
        wvq.wait(te, wtn)
        for tt in range(NTT):
            bank = pmm[tt % 2]
            if (oc, tt) not in ((0, 0), (0, 1)):
                pi = (oc, tt - 2) if tt >= 2 else (oc - 1, NTT - 2 + tt)
                te.wait_ge(ev.h, vt_ev[pi])
            for dc in range(NDC):
                ins = te.matmul(bank[:, :],
                                x2T[:, dc * TPC + tt * 128: dc * TPC + tt * 128 + 128],
                                wvsl[:, dc * 512:(dc + 1) * 512],
                                start=(dc == 0), stop=(dc == NDC - 1))
            mmn = mm.inc(ins)
            v.wait_ge(mm.h, mmn)
            if (oc, tt) not in ((0, 0), (0, 1)):
                pi = (oc, tt - 2) if tt >= 2 else (oc - 1, NTT - 2 + tt)
                vst.wait(v, vt_st[pi])
            vt_ev[(oc, tt)] = ev.inc(v.tensor_copy(vsm[tt % 2][:, :], bank[:, :]))
            sy.wait_ge(ev.h, vt_ev[(oc, tt)])
            vt_st[(oc, tt)] = vst.dma(sy,  tt,
                vt[tt * 128:(tt + 1) * 128, oc * 512:(oc + 1) * 512],
                vsm[tt % 2][:, :])

    stq.wait_all(sy)
    vst.wait_all(sy)
    vst.wait_all(g)
    return c.finish()


def build_attn():
    c = Ctx()
    nc = c.nc
    qt = nc.dram_tensor("qt", [D, TPC], F32, kind="ExternalInput").ap()
    ktf = nc.dram_tensor("ktf", [KD, T], F32, kind="ExternalInput").ap()
    vtf = nc.dram_tensor("vtf", [T, KD], F32, kind="ExternalInput").ap()
    maskb = nc.dram_tensor("maskb", [TPC, T], BF16, kind="ExternalInput").ap()
    xin = nc.dram_tensor("xin", [TPC, D], F32, kind="ExternalInput").ap()
    wo_r = nc.dram_tensor("wo_r", [8, 128, NDC * 512], F32, kind="ExternalInput").ap()
    xout = nc.dram_tensor("xout", [TPC, D], F32, kind="ExternalOutput").ap()
    ctxd = nc.dram_tensor("ctxd", [D, TPC], BF16, kind="Internal").ap()
    ident_d = nc.inline_tensor(np.eye(128, dtype=BF), "ident_c").ap()

    NKC = T // 128
    ident = c.sb("ident", [128, 128], BF16)
    astat = c.sb("astat", [128, 8], F32)

    ld = c.sem("ld"); mm = c.sem("mm")
    ev = c.sem("ev"); ac = c.sem("ac")
    ldg = c.sem("ldg")
    mq = QSem(c, "mld", 2)
    cst = QSem(c, "cst", 2)
    w1q = QSem(c, "w1q", 1)
    y1q = QSem(c, "y1q", 2)
    p3x = QSem(c, "p3x", 2)
    p3y = QSem(c, "p3y", 2)
    xoq = QSem(c, "xoq", 2)
    consts = Consts(c, None)
    c.begin()
    g, v, s, te, sy = nc.gpsimd, nc.vector, nc.scalar, nc.tensor, nc.sync
    consts.ev = ev
    eps_ap = consts.ap(nc, EPS)
    consts.seal(nc)
    alleng = [g, v, s, te, sy]

    ph = Sub(nc)
    qt_sb = ph.sb("qt_sb", [128, NH * TPC], BF16)
    ktf_sb = ph.sb("ktf_sb", [128, NKV * T], BF16)
    vtf_sb = ph.sb("vtf_sb", [128, NKC * KD], BF16)
    msk = [ph.sb(f"msk{i}", [128, T], BF16) for i in range(2)]
    probs = ph.sb("probs", [128, T], BF16)
    probsT = ph.sb("probsT", [128, T], BF16)
    ctxsm = [ph.sb(f"ctxsm{i}", [128, 128], BF16) for i in range(2)]
    pS = ph.ps("pS", [128, T], F32)
    ptr = [ph.ps(f"ptr{i}", [128, 1024], BF16) for i in range(2)]
    pctx = ph.ps("pctx", [128, 128], F32)

    ld.inc(sy.dma_start(out=ident[:, :], in_=ident_d), 16)
    ldg.inc(g.dma_start(out=qt_sb[:, :].rearrange("p (c t) -> p c t", c=NH),
                        in_=qt.rearrange("(c p) t -> p c t", p=128)), 16)
    ldg.inc(g.dma_start(out=ktf_sb[:, :].rearrange("p (c t) -> p c t", c=NKV),
                        in_=ktf.rearrange("(c p) t -> p c t", p=128)), 16)
    ldg.inc(g.dma_start(out=vtf_sb[:, :].rearrange("p (c t) -> p c t", c=NKC),
                        in_=vtf.rearrange("(c p) t -> p c t", p=128)), 16)
    te.wait_ge(ld.h, ld.n)
    te.wait_ge(ldg.h, ldg.n)

    negmax = astat[:, 0:1]; sumexp = astat[:, 1:2]; recip = astat[:, 2:3]
    ac_exp = {}
    mm_tr = {}
    ev_ptc = {}
    st_ctx = {}
    mm_pv = {}
    ld_msk = {}
    for tt in range(NTT):
        if tt >= 2:
            sy.wait_ge(mm.h, mm_pv[(tt - 2, NH - 1)])
        ld_msk[tt] = mq.dma(sy, tt, msk[tt % 2][:, :],
                            maskb[tt * 128:(tt + 1) * 128, :])
        for h in range(NH):
            kvh = h // 4
            mq.wait(te, ld_msk[tt])
            if not (tt == 0 and h == 0):
                te.wait_ge(ac.h, ac.n)
            for cb in range(T // 512):
                te.matmul(pS[:, cb * 512:(cb + 1) * 512], ident[:, :],
                          msk[tt % 2][:, cb * 512:(cb + 1) * 512],
                          start=True, stop=False)
                ins = te.matmul(pS[:, cb * 512:(cb + 1) * 512],
                                qt_sb[:, h * TPC + tt * 128: h * TPC + tt * 128 + 128],
                                ktf_sb[:, kvh * T + cb * 512: kvh * T + cb * 512 + 512],
                                start=False, stop=True)
            mmn = mm.inc(ins)
            v.wait_ge(mm.h, mmn)
            evn = ev.inc(v.reduce_max(negmax, pS[:, :], axis=AX.X, negate=True))
            s.wait_ge(ev.h, evn)
            if not (tt == 0 and h == 0):
                s.wait_ge(mm.h, mm_tr[(tt, h - 1) if h else (tt - 1, NH - 1)])
            ac_exp[(tt, h)] = ac.inc(s.activation(probs[:, :], pS[:, :], AF.Exp,
                                                  bias=negmax, scale=1.0,
                                                  accum_out=sumexp))
            v.wait_ge(ac.h, ac_exp[(tt, h)])
            evr = ev.inc(v.reciprocal(recip, sumexp))
            v.wait_ge(ev.h, evr)
            evs = ev.inc(v.tensor_scalar(probs[:, :], probs[:, :], recip, 0.0,
                                         ALU.mult, ALU.add))
            te.wait_ge(ev.h, evs)
            for gr2 in range(2):
                bank = ptr[gr2]
                if not (tt == 0 and h == 0):
                    pk = (tt, h - 1) if h else (tt - 1, NH - 1)
                    te.wait_ge(ev.h, ev_ptc[(pk, gr2)])
                for k in range(8):
                    kc = gr2 * 8 + k
                    ins = te.transpose(bank[:, k * 128:(k + 1) * 128],
                                       probs[:, kc * 128:(kc + 1) * 128], ident[:, :])
                trn = mm.inc(ins)
                v.wait_ge(mm.h, trn)
                ev_ptc[((tt, h), gr2)] = ev.inc(v.tensor_copy(
                    probsT[:, gr2 * 1024:(gr2 + 1) * 1024], bank[:, :]))
            mm_tr[(tt, h)] = mm.n
            te.wait_ge(ev.h, ev_ptc[((tt, h), 1)])
            for kc in range(NKC):
                ins = te.matmul(pctx[:, :],
                                vtf_sb[:, kc * KD + kvh * 128: kc * KD + kvh * 128 + 128],
                                probsT[:, kc * 128:(kc + 1) * 128],
                                start=(kc == 0), stop=(kc == NKC - 1))
            mm_pv[(tt, h)] = mm.inc(ins)
            v.wait_ge(mm.h, mm_pv[(tt, h)])
            if not (tt == 0 and h <= 1):
                pk = (tt, h - 2) if h >= 2 else (tt - 1, NH - 2 + h)
                cst.wait(v, st_ctx[pk])
            evc = ev.inc(v.tensor_copy(ctxsm[h % 2][:, :], pctx[:, :]))
            g.wait_ge(ev.h, evc)
            st_ctx[(tt, h)] = cst.dma(g, h,
                ctxd[h * 128:(h + 1) * 128, tt * 128:(tt + 1) * 128],
                ctxsm[h % 2][:, :])
    fence(nc, c.sems, alleng)
    ph.close()

    y1d = nc.dram_tensor("y1d", [TPC, D], F32, kind="Internal").ap()
    p2 = Sub(nc)
    ctxT = p2.sb("ctxT", [128, NDC * TPC], BF16)
    wsl = p2.sb("wsl", [128, NDC * 512], BF16)
    ysm = [p2.sb(f"ysm{i}", [128, 512], F32) for i in range(2)]
    pmm = [p2.ps(f"pmm{i}", [128, 512], F32) for i in range(2)]

    ld.inc(sy.dma_start(out=ctxT[:, :].rearrange("p (c t) -> p c t", c=NDC),
                       in_=ctxd.rearrange("(c p) t -> p c t", p=128)), 16)
    ctx_ld = ld.n
    te.wait_ge(ld.h, ctx_ld)
    ev_y1 = {}
    st_y1 = {}
    for oc in range(8):
        wtn = w1q.dma(g, 0, wsl[:, :], wo_r[oc])
        w1q.wait(te, wtn)
        for tt in range(NTT):
            bank = pmm[tt % 2]
            if not (oc == 0 and tt <= 1):
                pk = (oc, tt - 2) if tt >= 2 else (oc - 1, NTT - 2 + tt)
                te.wait_ge(ev.h, ev_y1[pk])
            for dc in range(NDC):
                ins = te.matmul(bank[:, :],
                                ctxT[:, dc * TPC + tt * 128: dc * TPC + tt * 128 + 128],
                                wsl[:, dc * 512:(dc + 1) * 512],
                                start=(dc == 0), stop=(dc == NDC - 1))
            mmn = mm.inc(ins)
            v.wait_ge(mm.h, mmn)
            if not (oc == 0 and tt <= 1):
                pk = (oc, tt - 2) if tt >= 2 else (oc - 1, NTT - 2 + tt)
                y1q.wait(v, st_y1[pk])
            ev_y1[(oc, tt)] = ev.inc(v.tensor_copy(ysm[tt % 2][:, :], bank[:, :]))
            sy.wait_ge(ev.h, ev_y1[(oc, tt)])
            st_y1[(oc, tt)] = y1q.dma(sy, tt,
                y1d[tt * 128:(tt + 1) * 128, oc * 512:(oc + 1) * 512],
                ysm[tt % 2][:, :])
        g.wait_ge(mm.h, mm.n)
    fence(nc, c.sems, alleng)
    p2.close()

    p3 = Sub(nc)
    xt = [p3.sb(f"at_xt{i}", [128, D], F32) for i in range(2)]
    y1t = [p3.sb(f"at_y1t{i}", [128, D], F32) for i in range(2)]
    scr = p3.sb("at_scr", [128, D], BF16)
    scr2 = p3.sb("at_scr2", [128, D], BF16)
    y1n = p3.sb("at_y1n", [128, D], F32)
    stat = p3.sb("at_stat", [128, 8], F32)

    st_fin = {}
    ev_fin = {}
    for tt in range(NTT):
        if tt >= 2:
            xoq.wait(sy, st_fin[tt - 2])
            sy.wait_ge(ev.h, ev_fin[tt - 2])
        tk1 = p3x.dma(sy, tt, xt[tt % 2][:, :], xin[tt * 128:(tt + 1) * 128, :])
        tk2 = p3y.dma(sy, tt, y1t[tt % 2][:, :], y1d[tt * 128:(tt + 1) * 128, :])
        sl = (tt % 2) * 4
        mean, mean2 = stat[:, sl:sl + 1], stat[:, sl + 1:sl + 2]
        rstd, nmb = stat[:, sl + 2:sl + 3], stat[:, sl + 3:sl + 4]
        for e_ in (v, s):
            p3x.wait(e_, tk1)
            p3y.wait(e_, tk2)
        evn = emit_ln_stats(nc, y1t[tt % 2][:, :], scr[:, :], scr2[:, :], mean,
                            mean2, rstd, nmb, ev, ac, eps_ap)
        s.wait_ge(ev.h, evn)
        if tt >= 1:
            s.wait_ge(ev.h, ev_fin[tt - 1])
        acn = ac.inc(s.activation(y1n[:, :], y1t[tt % 2][:, :], AF.Identity,
                                  bias=nmb, scale=rstd))
        v.wait_ge(ac.h, acn)
        ev_fin[tt] = ev.inc(v.tensor_tensor(xt[tt % 2][:, :], xt[tt % 2][:, :],
                                            y1n[:, :], ALU.add))
        sy.wait_ge(ev.h, ev_fin[tt])
        st_fin[tt] = xoq.dma(sy, tt, xout[tt * 128:(tt + 1) * 128, :],
                             xt[tt % 2][:, :])
    xoq.wait_all(sy)
    xoq.wait_all(g)
    p3.close()
    return c.finish()


def emit_small_chain(nc, ops, ev):
    v = nc.vector
    last = None
    for f in ops:
        if last is not None:
            v.wait_ge(ev.h, last)
        last = ev.inc(f(v))
    return last


def build_ffn(layer):
    c = Ctx()
    nc = c.nc
    xin = nc.dram_tensor("xin", [TPC, D], F32, kind="ExternalInput").ap()
    bw_r = nc.dram_tensor("bw_r", [8, 128, NDC * 512], F32, kind="ExternalInput").ap()
    rt_w = nc.dram_tensor("rt_w", [128, NDC * 8], F32, kind="ExternalInput").ap()
    a2_w = nc.dram_tensor("a2_w", [128, NDC * 128], F32, kind="ExternalInput").ap()
    b2_w = nc.dram_tensor("b2_w", [128, D], F32, kind="ExternalInput").ap()
    if layer == 1:
        bias_w = nc.dram_tensor("bias_w", [1, D], F32, kind="ExternalInput").ap()
        thr_w = nc.dram_tensor("thr_w", [128, NDC], F32, kind="ExternalInput").ap()
        bthr = nc.dram_tensor("bthr", [1, 1], F32, kind="ExternalInput").ap()
    xout = nc.dram_tensor("xout", [TPC, D], F32, kind="ExternalOutput").ap()
    y2d = nc.dram_tensor("y2d", [TPC, D], F32, kind="Internal").ap()
    ident_d = nc.inline_tensor(np.eye(128, dtype=BF), "ident_c").ap()
    ones_d = nc.inline_tensor(np.ones((1, 128), dtype=BF), "ones_c").ap()
    onesf_d = nc.inline_tensor(np.ones((1, 128), dtype=np.float32), "onesf_c").ap()

    x4T = c.sb("x4T", [128, NDC * TPC], BF16)
    ident = c.sb("ident", [128, 128], BF16)
    ones_sb = c.sb("ones_sb", [1, 128], BF16)
    rt_sb = c.sb("rt_sb", [128, NDC * 8], BF16)
    a2_sb = c.sb("a2_sb", [128, NDC * 128], BF16)
    b2_sb = c.sb("b2_sb", [128, D], BF16)
    midT = c.sb("midT", [128, NTT * 128], BF16)
    fstat = c.sb("fstat", [128, 16], F32)
    g8 = c.sb("g8", [128, 8], F32)
    w8 = c.sb("w8", [128, 8], F32)
    t8 = c.sb("t8", [128, 8], F32)
    wn = c.sb("wn", [128, 8], F32)
    midf = c.sb("midf", [128, 128], F32)
    midb = c.sb("midb", [128, 128], BF16)
    if layer == 1:
        bias_sb = c.sb("bias_sb", [1, D], BF16)
        thr_sb = c.sb("thr_sb", [128, NDC], BF16)
        bthr_sb = c.sb("bthr_sb", [1, 1], F32)
        bthr_bc = c.sb("bthr_bc", [128, 1], F32)
        onesf_sb = c.sb("onesf_sb", [1, 128], F32)

    ld = c.sem("ld"); mm = c.sem("mm")
    ev = c.sem("ev"); ac = c.sem("ac")
    lds = c.sem("lds")
    xq = QSem(c, "xld", 2)
    bwq = QSem(c, "bwq", 2)
    y2q = QSem(c, "y2q", 2)
    p3x = QSem(c, "p3x", 2)
    p3y = QSem(c, "p3y", 2)
    xoq = QSem(c, "xoq", 2)
    consts = Consts(c, None)
    c.begin()
    g, v, s, te, sy = nc.gpsimd, nc.vector, nc.scalar, nc.tensor, nc.sync
    consts.ev = ev
    eps_ap = consts.ap(nc, EPS)
    consts.seal(nc)
    alleng = [g, v, s, te, sy]

    lds.inc(sy.dma_start(out=ident[:, :], in_=ident_d), 16)
    lds.inc(sy.dma_start(out=ones_sb[:, :], in_=ones_d), 16)
    ld.inc(g.dma_start(out=rt_sb[:, :], in_=rt_w), 16)
    ld.inc(g.dma_start(out=a2_sb[:, :], in_=a2_w), 16)
    ld.inc(g.dma_start(out=b2_sb[:, :], in_=b2_w), 16)
    if layer == 1:
        ld.inc(g.dma_start(out=bias_sb[:, :], in_=bias_w), 16)
        ld.inc(g.dma_start(out=thr_sb[:, :], in_=thr_w), 16)
        ld.inc(g.dma_start(out=bthr_sb[:, :], in_=bthr), 16)
        lds.inc(sy.dma_start(out=onesf_sb[:, :], in_=onesf_d), 16)
    const_ld = ld.n
    const_lds = lds.n
    for e in (te, v, s):
        e.wait_ge(ld.h, const_ld)
        e.wait_ge(lds.h, const_lds)
    if layer == 1:
        pbc = c.ps("pbc", [128, 1], F32)
        mmb = mm.inc(te.matmul(pbc[:, :], onesf_sb[0:1, :], bthr_sb[0:1, 0:1],
                               start=True, stop=True))
        v.wait_ge(mm.h, mmb)
        ev.inc(v.tensor_copy(bthr_bc[:, :], pbc[:, :]))
        s.wait_ge(ev.h, ev.n)

    pa = Sub(nc)
    x4T_ev = emit_ln_transpose_phase2(nc, c, pa, xin, x4T, ev, ac, mm, xq, ident,
                                      eps_ap)
    fence(nc, c.sems, alleng)
    pa.close()

    pb = Sub(nc)
    plog = pb.ps("plog", [128, 16], F32)
    pmid = pb.ps("pmid", [128, 128], F32)
    pmidT = pb.ps("pmidT", [128, 128], BF16)
    negm = fstat[:, 0:1]; sumg = fstat[:, 1:2]; rcp = fstat[:, 2:3]
    m1 = fstat[:, 3:4]; m2 = fstat[:, 4:5]; wsum = fstat[:, 5:6]
    rcpw = fstat[:, 6:7]; thrv = fstat[:, 7:8]; thr2 = fstat[:, 8:9]

    for tt in range(NTT):
        if tt > 0:
            te.wait_ge(ac.h, ac.n)
        for dc in range(NDC):
            ins = te.matmul(plog[:, 0:8],
                            x4T[:, dc * TPC + tt * 128: dc * TPC + tt * 128 + 128],
                            rt_sb[:, dc * 8:(dc + 1) * 8],
                            start=(dc == 0), stop=(dc == NDC - 1))
        if layer == 1:
            for dc in range(NDC):
                ins = te.matmul(plog[:, 8:9],
                                x4T[:, dc * TPC + tt * 128: dc * TPC + tt * 128 + 128],
                                thr_sb[:, dc:dc + 1],
                                start=(dc == 0), stop=(dc == NDC - 1))
        mml = mm.inc(ins)
        v.wait_ge(mm.h, mml)
        ev1 = ev.inc(v.reduce_max(negm, plog[:, 0:8], axis=AX.X, negate=True))
        s.wait_ge(ev.h, ev1)
        if tt > 0:
            s.wait_ge(ev.h, ev_seldone)
        ac1 = ac.inc(s.activation(g8[:, :], plog[:, 0:8], AF.Exp, bias=negm,
                                  scale=1.0, accum_out=sumg))
        if layer == 1:
            ac2 = ac.inc(s.activation(thrv, plog[:, 8:9], AF.Sigmoid,
                                      bias=bthr_bc[:, :], scale=1.0))
        v.wait_ge(ac.h, ac.n)
        if layer == 0:
            ev_seldone = emit_small_chain(nc, [
                lambda vv: vv.reciprocal(rcp, sumg),
                lambda vv: vv.tensor_scalar(g8[:, :], g8[:, :], rcp, 0.0, ALU.mult, ALU.add),
                lambda vv: vv.reduce_max(m1, g8[:, :], axis=AX.X),
                lambda vv: vv.tensor_scalar(t8[:, :], g8[:, :], m1, 0.0, ALU.is_ge, ALU.add),
                lambda vv: vv.tensor_scalar(t8[:, :], t8[:, :], -1e9, 0.0, ALU.mult, ALU.add),
                lambda vv: vv.tensor_tensor(t8[:, :], g8[:, :], t8[:, :], ALU.add),
                lambda vv: vv.reduce_max(m2, t8[:, :], axis=AX.X),
                lambda vv: vv.tensor_scalar(t8[:, :], g8[:, :], m2, 0.0, ALU.is_ge, ALU.add),
                lambda vv: vv.tensor_tensor(w8[:, :], g8[:, :], t8[:, :], ALU.mult),
                lambda vv: vv.tensor_scalar(t8[:, :], w8[:, :], 1.0, 0.0, ALU.mult,
                                            ALU.add, accum_out=wsum),
                lambda vv: vv.reciprocal(rcpw, wsum),
                lambda vv: vv.tensor_scalar(wn[:, :], w8[:, :], rcpw, 2.0, ALU.mult, ALU.mult),
            ], ev)
        else:
            ev_seldone = emit_small_chain(nc, [
                lambda vv: vv.reciprocal(rcp, sumg),
                lambda vv: vv.tensor_scalar(g8[:, :], g8[:, :], rcp, 0.0, ALU.mult, ALU.add),
                lambda vv: vv.tensor_scalar(thr2, thrv, 0.125, 0.0, ALU.mult, ALU.add),
                lambda vv: vv.tensor_scalar(t8[:, :], g8[:, :], thr2, 0.0, ALU.subtract, ALU.add),
                lambda vv: vv.tensor_scalar(w8[:, :], t8[:, :], 0.0, 0.0, ALU.max, ALU.add),
                lambda vv: vv.tensor_scalar(t8[:, :], w8[:, :], 1.0, 0.0, ALU.mult,
                                            ALU.add, accum_out=wsum),
                lambda vv: vv.tensor_scalar(wsum, wsum, 1e-30, 0.0, ALU.max, ALU.add),
                lambda vv: vv.reciprocal(rcpw, wsum),
                lambda vv: vv.tensor_scalar(wn[:, :], w8[:, :], rcpw, 2.0, ALU.mult, ALU.mult),
            ], ev)
        if tt > 0:
            te.wait_ge(ev.h, ev_midf)
        for dc in range(NDC):
            ins = te.matmul(pmid[:, :],
                            x4T[:, dc * TPC + tt * 128: dc * TPC + tt * 128 + 128],
                            a2_sb[:, dc * 128:(dc + 1) * 128],
                            start=(dc == 0), stop=(dc == NDC - 1))
        mmm = mm.inc(ins)
        v.wait_ge(mm.h, mmm)
        ev_midf = ev.inc(v.tensor_copy(midf[:, :], pmid[:, :]))
        v.wait_ge(ev.h, ev_midf)
        v.wait_ge(ev.h, ev_seldone)
        if tt > 0:
            v.wait_ge(mm.h, mm_midT)
        ev_midb = ev.inc(v.tensor_tensor(
            midb[:, :].rearrange("p (e r) -> p e r", e=NE),
            midf[:, :].rearrange("p (e r) -> p e r", e=NE),
            wn[:, :].unsqueeze(2).broadcast_to([128, NE, RK]),
            ALU.mult))
        te.wait_ge(ev.h, ev_midb)
        if tt > 0:
            te.wait_ge(ev.h, ev_midT)
        mm_midT = mm.inc(te.transpose(pmidT[:, :], midb[:, :], ident[:, :]))
        v.wait_ge(mm.h, mm_midT)
        ev_midT = ev.inc(v.tensor_copy(midT[:, tt * 128:(tt + 1) * 128], pmidT[:, :]))
    fence(nc, c.sems, alleng)
    pb.close()

    pc = Sub(nc)
    wsl = [pc.sb(f"ffn_wsl{i}", [128, NDC * 512], BF16) for i in range(2)]
    ysm = [pc.sb(f"ffn_ysm{i}", [128, 512], F32) for i in range(2)]
    pmm = [pc.ps(f"ffn_pmm{i}", [128, 512], F32) for i in range(2)]
    ev_y2 = {}
    st_y2 = {}
    wt_sl = {}
    mm_oc = {}
    for oc in range(8):
        if oc >= 2:
            g.wait_ge(mm.h, mm_oc[oc - 2])
        wt_sl[oc] = bwq.dma(g, oc, wsl[oc % 2][:, :], bw_r[oc])
        bwq.wait(te, wt_sl[oc])
        for tt in range(NTT):
            bank = pmm[tt % 2]
            if not (oc == 0 and tt <= 1):
                pk = (oc, tt - 2) if tt >= 2 else (oc - 1, NTT - 2 + tt)
                te.wait_ge(ev.h, ev_y2[pk])
            for dc in range(NDC):
                te.matmul(bank[:, :],
                          x4T[:, dc * TPC + tt * 128: dc * TPC + tt * 128 + 128],
                          wsl[oc % 2][:, dc * 512:(dc + 1) * 512],
                          start=(dc == 0), stop=False)
            if layer == 1:
                te.matmul(bank[:, :], ones_sb[0:1, :],
                          bias_sb[0:1, oc * 512:(oc + 1) * 512],
                          start=False, stop=False)
            ins = te.matmul(bank[:, :], midT[:, tt * 128:(tt + 1) * 128],
                            b2_sb[:, oc * 512:(oc + 1) * 512],
                            start=False, stop=True)
            mmn = mm.inc(ins)
            if tt == NTT - 1:
                mm_oc[oc] = mmn
            v.wait_ge(mm.h, mmn)
            if not (oc == 0 and tt <= 1):
                pk = (oc, tt - 2) if tt >= 2 else (oc - 1, NTT - 2 + tt)
                y2q.wait(v, st_y2[pk])
            ev_y2[(oc, tt)] = ev.inc(v.tensor_copy(ysm[tt % 2][:, :], bank[:, :]))
            sy.wait_ge(ev.h, ev_y2[(oc, tt)])
            st_y2[(oc, tt)] = y2q.dma(sy, tt,
                y2d[tt * 128:(tt + 1) * 128, oc * 512:(oc + 1) * 512],
                ysm[tt % 2][:, :])
    fence(nc, c.sems, alleng)
    pc.close()

    pd = Sub(nc)
    xt = [pd.sb(f"fd_xt{i}", [128, D], F32) for i in range(2)]
    y2t = [pd.sb(f"fd_y2t{i}", [128, D], F32) for i in range(2)]
    scr = pd.sb("fd_scr", [128, D], BF16)
    scr2 = pd.sb("fd_scr2", [128, D], BF16)
    y2n = pd.sb("fd_y2n", [128, D], F32)
    stat = pd.sb("fd_stat", [128, 8], F32)
    st_fin = {}
    ev_fin = {}
    for tt in range(NTT):
        if tt >= 2:
            xoq.wait(sy, st_fin[tt - 2])
            sy.wait_ge(ev.h, ev_fin[tt - 2])
        tk1 = p3x.dma(sy, tt, xt[tt % 2][:, :], xin[tt * 128:(tt + 1) * 128, :])
        tk2 = p3y.dma(sy, tt, y2t[tt % 2][:, :], y2d[tt * 128:(tt + 1) * 128, :])
        sl = (tt % 2) * 4
        mean, mean2 = stat[:, sl:sl + 1], stat[:, sl + 1:sl + 2]
        rstd, nmb = stat[:, sl + 2:sl + 3], stat[:, sl + 3:sl + 4]
        for e_ in (v, s):
            p3x.wait(e_, tk1)
            p3y.wait(e_, tk2)
        evn = emit_ln_stats(nc, y2t[tt % 2][:, :], scr[:, :], scr2[:, :], mean,
                            mean2, rstd, nmb, ev, ac, eps_ap)
        s.wait_ge(ev.h, evn)
        if tt >= 1:
            s.wait_ge(ev.h, ev_fin[tt - 1])
        acn = ac.inc(s.activation(y2n[:, :], y2t[tt % 2][:, :], AF.Identity,
                                  bias=nmb, scale=rstd))
        v.wait_ge(ac.h, acn)
        ev_fin[tt] = ev.inc(v.tensor_tensor(xt[tt % 2][:, :], xt[tt % 2][:, :],
                                            y2n[:, :], ALU.add))
        sy.wait_ge(ev.h, ev_fin[tt])
        st_fin[tt] = xoq.dma(sy, tt, xout[tt * 128:(tt + 1) * 128, :],
                             xt[tt % 2][:, :])
    xoq.wait_all(sy)
    xoq.wait_all(g)
    pd.close()
    return c.finish()


def emit_ln_transpose_phase2(nc, c, sub, xin_ap, x2T, ev, ac, mm, xq, ident, eps_ap,
                             pfx="ln2"):
    g, v, s, te = nc.gpsimd, nc.vector, nc.scalar, nc.tensor
    xt = [sub.sb(f"{pfx}_xt0", [128, D], F32), sub.sb(f"{pfx}_xt1", [128, D], F32)]
    scr = sub.sb(f"{pfx}_scr", [128, D], BF16)
    scr2 = sub.sb(f"{pfx}_scr2", [128, D], BF16)
    stat = sub.sb(f"{pfx}_stat", [128, 8], F32)
    x2 = [sub.sb(f"{pfx}_x2a", [128, D], BF16), sub.sb(f"{pfx}_x2b", [128, D], BF16)]
    ptr = [sub.ps(f"{pfx}_ptr0", [128, 1024], BF16), sub.ps(f"{pfx}_ptr1", [128, 1024], BF16)]

    ld_tile = []
    ac_norm = {}
    ev_copy = {}
    mm_tr = {}
    x2T3 = x2T[:, :].rearrange("p (c t) -> p c t", c=NDC)
    for i in range(NTT):
        if i >= 2:
            g.wait_ge(ac.h, ac_norm[i - 2])
        ld_tile.append(xq.dma(g, i, xt[i % 2][:, :],
                              xin_ap[i * 128:(i + 1) * 128, :]))
        sl = (i % 2) * 4
        mean, mean2 = stat[:, sl:sl + 1], stat[:, sl + 1:sl + 2]
        rstd, nmb = stat[:, sl + 2:sl + 3], stat[:, sl + 3:sl + 4]
        xq.wait(v, ld_tile[i])
        xq.wait(s, ld_tile[i])
        evn = emit_ln_stats(nc, xt[i % 2][:, :], scr[:, :], scr2[:, :], mean,
                            mean2, rstd, nmb, ev, ac, eps_ap)
        s.wait_ge(ev.h, evn)
        if i >= 2:
            s.wait_ge(mm.h, mm_tr[(i - 2, 3)])
        ac_norm[i] = ac.inc(s.activation(x2[i % 2][:, :], xt[i % 2][:, :],
                                         AF.Identity, bias=nmb, scale=rstd))
        te.wait_ge(ac.h, ac_norm[i])
        for gr in range(4):
            bank = ptr[gr % 2]
            if (i, gr) not in ((0, 0), (0, 1)):
                prev = (i, gr - 2) if gr >= 2 else (i - 1, gr + 2)
                te.wait_ge(ev.h, ev_copy[prev])
            for k in range(8):
                dc = gr * 8 + k
                ins = te.transpose(bank[:, k * 128:(k + 1) * 128],
                                   x2[i % 2][:, dc * 128:(dc + 1) * 128],
                                   ident[:, :])
            mm_tr[(i, gr)] = mm.inc(ins)
            v.wait_ge(mm.h, mm_tr[(i, gr)])
            dst = x2T3[:, gr * 8:(gr + 1) * 8, i * 128:(i + 1) * 128]
            src = bank[:, :].rearrange("p (c t) -> p c t", c=8)
            ev_copy[(i, gr)] = ev.inc(v.tensor_copy(dst, src))
    return ev.n


MAGIC = 305419896


def emit_outln_residual(nc, c, xin_ap, yd, xo, p3x, p3y, xoq_, ev, ac, eps_ap,
                        pfx="o"):
    g, v, s, te, sy = nc.gpsimd, nc.vector, nc.scalar, nc.tensor, nc.sync
    p3 = Sub(nc)
    xt = [p3.sb(f"{pfx}_xt{i}", [128, D], F32) for i in range(2)]
    yt = [p3.sb(f"{pfx}_yt{i}", [128, D], F32) for i in range(2)]
    scr = p3.sb(f"{pfx}_scr", [128, D], BF16)
    scr2 = p3.sb(f"{pfx}_scr2", [128, D], BF16)
    yn = p3.sb(f"{pfx}_yn", [128, D], F32)
    stat = p3.sb(f"{pfx}_stat", [128, 8], F32)
    st_fin = {}
    ev_fin = {}
    for tt in range(NTT):
        if tt >= 2:
            xoq_.wait(sy, st_fin[tt - 2])
            sy.wait_ge(ev.h, ev_fin[tt - 2])
        tk1 = p3x.dma(sy, tt, xt[tt % 2][:, :], xin_ap[tt * 128:(tt + 1) * 128, :])
        tk2 = p3y.dma(sy, tt, yt[tt % 2][:, :], yd[tt * 128:(tt + 1) * 128, :])
        sl = (tt % 2) * 4
        mean, mean2 = stat[:, sl:sl + 1], stat[:, sl + 1:sl + 2]
        rstd, nmb = stat[:, sl + 2:sl + 3], stat[:, sl + 3:sl + 4]
        for e_ in (v, s):
            p3x.wait(e_, tk1)
            p3y.wait(e_, tk2)
        evn = emit_ln_stats(nc, yt[tt % 2][:, :], scr[:, :], scr2[:, :], mean,
                            mean2, rstd, nmb, ev, ac, eps_ap)
        s.wait_ge(ev.h, evn)
        if tt >= 1:
            s.wait_ge(ev.h, ev_fin[tt - 1])
        acn = ac.inc(s.activation(yn[:, :], yt[tt % 2][:, :], AF.Identity,
                                  bias=nmb, scale=rstd))
        v.wait_ge(ac.h, acn)
        ev_fin[tt] = ev.inc(v.tensor_tensor(xt[tt % 2][:, :], xt[tt % 2][:, :],
                                            yn[:, :], ALU.add))
        sy.wait_ge(ev.h, ev_fin[tt])
        st_fin[tt] = xoq_.dma(sy, tt, xo[tt * 128:(tt + 1) * 128, :],
                              xt[tt % 2][:, :])
    xoq_.wait_all(sy)
    p3.close()


def emit_ffn_body(nc, c, layer, x1d, xout, y2d, x2Tname, rt_w, a2_w, b2_w,
                  bw_r, ident, ones_sb, ffn_extra, mm, ev, ac, ldg, xq, bwq,
                  y2q, f3x, f3y, xoq, eps_ap, alleng, c_, pfx="f"):
    g, v, s, te, sy = nc.gpsimd, nc.vector, nc.scalar, nc.tensor, nc.sync
    sG = Sub(nc)
    x4T = sG.sb(x2Tname, [128, NDC * TPC], BF16)
    rt_sb = sG.sb(f"{pfx}_rt", [128, NDC * 8], BF16)
    a2_sb = sG.sb(f"{pfx}_a2", [128, NDC * 128], BF16)
    b2_sb = sG.sb(f"{pfx}_b2", [128, D], BF16)
    midT = sG.sb(f"{pfx}_midT", [128, NTT * 128], BF16)
    fstat = sG.sb(f"{pfx}_fstat", [128, 16], F32)
    g8 = sG.sb(f"{pfx}_g8", [128, 8], F32)
    w8 = sG.sb(f"{pfx}_w8", [128, 8], F32)
    t8 = sG.sb(f"{pfx}_t8", [128, 8], F32)
    wn = sG.sb(f"{pfx}_wn", [128, 8], F32)
    midf = sG.sb(f"{pfx}_midf", [128, 128], F32)
    midb = sG.sb(f"{pfx}_midb", [128, 128], BF16)

    ldg.inc(g.dma_start(out=rt_sb[:, :], in_=rt_w), 16)
    ldg.inc(g.dma_start(out=a2_sb[:, :], in_=a2_w), 16)
    ldg.inc(g.dma_start(out=b2_sb[:, :], in_=b2_w), 16)
    te.wait_ge(ldg.h, ldg.n)

    sGa = Sub(nc)
    x4T_ev = emit_ln_transpose_phase2(nc, c_, sGa, x1d, x4T, ev, ac, mm, xq,
                                      ident, eps_ap, pfx=pfx + "_ln4")
    fence(nc, c_.sems, alleng)
    sGa.close()

    sH = Sub(nc)
    plog = sH.ps(f"{pfx}_plog", [128, 16], F32)
    pmid = sH.ps(f"{pfx}_pmid", [128, 128], F32)
    pmidT = sH.ps(f"{pfx}_pmidT", [128, 128], BF16)
    negm = fstat[:, 0:1]; sumg = fstat[:, 1:2]; rcp = fstat[:, 2:3]
    m1 = fstat[:, 3:4]; m2 = fstat[:, 4:5]; wsum = fstat[:, 5:6]
    rcpw = fstat[:, 6:7]; thrv = fstat[:, 7:8]; thr2 = fstat[:, 8:9]
    ev_seldone = None
    ev_midf = None
    mm_midT = None
    ev_midT = None
    for tt in range(NTT):
        if tt > 0:
            te.wait_ge(ac.h, ac.n)
        for dc in range(NDC):
            ins = te.matmul(plog[:, 0:8],
                            x4T[:, dc * TPC + tt * 128: dc * TPC + tt * 128 + 128],
                            rt_sb[:, dc * 8:(dc + 1) * 8],
                            start=(dc == 0), stop=(dc == NDC - 1))
        if layer == 1:
            for dc in range(NDC):
                ins = te.matmul(plog[:, 8:9],
                                x4T[:, dc * TPC + tt * 128: dc * TPC + tt * 128 + 128],
                                ffn_extra["thr_sb"][:, dc:dc + 1],
                                start=(dc == 0), stop=(dc == NDC - 1))
        mml = mm.inc(ins)
        v.wait_ge(mm.h, mml)
        ev1 = ev.inc(v.reduce_max(negm, plog[:, 0:8], axis=AX.X, negate=True))
        s.wait_ge(ev.h, ev1)
        if tt > 0:
            s.wait_ge(ev.h, ev_seldone)
        ac1 = ac.inc(s.activation(g8[:, :], plog[:, 0:8], AF.Exp, bias=negm,
                                  scale=1.0, accum_out=sumg))
        if layer == 1:
            ac1 = ac.inc(s.activation(thrv, plog[:, 8:9], AF.Sigmoid,
                                      bias=ffn_extra["bthr_bc"][:, :], scale=1.0))
        v.wait_ge(ac.h, ac.n)
        if layer == 0:
            ev_seldone = emit_small_chain(nc, [
                lambda vv: vv.reciprocal(rcp, sumg),
                lambda vv: vv.tensor_scalar(g8[:, :], g8[:, :], rcp, 0.0, ALU.mult, ALU.add),
                lambda vv: vv.reduce_max(m1, g8[:, :], axis=AX.X),
                lambda vv: vv.tensor_scalar(t8[:, :], g8[:, :], m1, 0.0, ALU.is_ge, ALU.add),
                lambda vv: vv.tensor_scalar(t8[:, :], t8[:, :], -1e9, 0.0, ALU.mult, ALU.add),
                lambda vv: vv.tensor_tensor(t8[:, :], g8[:, :], t8[:, :], ALU.add),
                lambda vv: vv.reduce_max(m2, t8[:, :], axis=AX.X),
                lambda vv: vv.tensor_scalar(t8[:, :], g8[:, :], m2, 0.0, ALU.is_ge, ALU.add),
                lambda vv: vv.tensor_tensor(w8[:, :], g8[:, :], t8[:, :], ALU.mult),
                lambda vv: vv.tensor_scalar(t8[:, :], w8[:, :], 1.0, 0.0, ALU.mult,
                                            ALU.add, accum_out=wsum),
                lambda vv: vv.reciprocal(rcpw, wsum),
                lambda vv: vv.tensor_scalar(wn[:, :], w8[:, :], rcpw, 2.0, ALU.mult, ALU.mult),
            ], ev)
        else:
            ev_seldone = emit_small_chain(nc, [
                lambda vv: vv.reciprocal(rcp, sumg),
                lambda vv: vv.tensor_scalar(g8[:, :], g8[:, :], rcp, 0.0, ALU.mult, ALU.add),
                lambda vv: vv.tensor_scalar(thr2, thrv, 0.125, 0.0, ALU.mult, ALU.add),
                lambda vv: vv.tensor_scalar(t8[:, :], g8[:, :], thr2, 0.0, ALU.subtract, ALU.add),
                lambda vv: vv.tensor_scalar(w8[:, :], t8[:, :], 0.0, 0.0, ALU.max, ALU.add),
                lambda vv: vv.tensor_scalar(t8[:, :], w8[:, :], 1.0, 0.0, ALU.mult,
                                            ALU.add, accum_out=wsum),
                lambda vv: vv.tensor_scalar(wsum, wsum, 1e-30, 0.0, ALU.max, ALU.add),
                lambda vv: vv.reciprocal(rcpw, wsum),
                lambda vv: vv.tensor_scalar(wn[:, :], w8[:, :], rcpw, 2.0, ALU.mult, ALU.mult),
            ], ev)
        if tt > 0:
            te.wait_ge(ev.h, ev_midf)
        for dc in range(NDC):
            ins = te.matmul(pmid[:, :],
                            x4T[:, dc * TPC + tt * 128: dc * TPC + tt * 128 + 128],
                            a2_sb[:, dc * 128:(dc + 1) * 128],
                            start=(dc == 0), stop=(dc == NDC - 1))
        mmm = mm.inc(ins)
        v.wait_ge(mm.h, mmm)
        ev_midf = ev.inc(v.tensor_copy(midf[:, :], pmid[:, :]))
        v.wait_ge(ev.h, ev_midf)
        v.wait_ge(ev.h, ev_seldone)
        if tt > 0:
            v.wait_ge(mm.h, mm_midT)
        ev_midb = ev.inc(v.tensor_tensor(
            midb[:, :].rearrange("p (e r) -> p e r", e=NE),
            midf[:, :].rearrange("p (e r) -> p e r", e=NE),
            wn[:, :].unsqueeze(2).broadcast_to([128, NE, RK]),
            ALU.mult))
        te.wait_ge(ev.h, ev_midb)
        if tt > 0:
            te.wait_ge(ev.h, ev_midT)
        mm_midT = mm.inc(te.transpose(pmidT[:, :], midb[:, :], ident[:, :]))
        v.wait_ge(mm.h, mm_midT)
        ev_midT = ev.inc(v.tensor_copy(midT[:, tt * 128:(tt + 1) * 128], pmidT[:, :]))
    fence(nc, c_.sems, alleng)
    sH.close()

    sI = Sub(nc)
    wsl = [sI.sb(f"{pfx}_i_wsl{i}", [128, NDC * 512], BF16) for i in range(2)]
    ysm = [sI.sb(f"{pfx}_i_ysm{i}", [128, 512], F32) for i in range(2)]
    ipmm = [sI.ps(f"{pfx}_i_pmm{i}", [128, 512], F32) for i in range(2)]
    ev_y2 = {}
    st_y2 = {}
    mm_oc = {}
    for oc in range(8):
        if oc >= 2:
            g.wait_ge(mm.h, mm_oc[oc - 2])
        wtok = bwq.dma(g, oc, wsl[oc % 2][:, :], bw_r[oc])
        bwq.wait(te, wtok)
        for tt in range(NTT):
            bank = ipmm[tt % 2]
            if not (oc == 0 and tt <= 1):
                pk = (oc, tt - 2) if tt >= 2 else (oc - 1, NTT - 2 + tt)
                te.wait_ge(ev.h, ev_y2[pk])
            for dc in range(NDC):
                te.matmul(bank[:, :],
                          x4T[:, dc * TPC + tt * 128: dc * TPC + tt * 128 + 128],
                          wsl[oc % 2][:, dc * 512:(dc + 1) * 512],
                          start=(dc == 0), stop=False)
            if layer == 1:
                te.matmul(bank[:, :], ones_sb[0:1, :],
                          ffn_extra["bias_sb"][0:1, oc * 512:(oc + 1) * 512],
                          start=False, stop=False)
            ins = te.matmul(bank[:, :], midT[:, tt * 128:(tt + 1) * 128],
                            b2_sb[:, oc * 512:(oc + 1) * 512],
                            start=False, stop=True)
            mmn = mm.inc(ins)
            if tt == NTT - 1:
                mm_oc[oc] = mmn
            v.wait_ge(mm.h, mmn)
            if not (oc == 0 and tt <= 1):
                pk = (oc, tt - 2) if tt >= 2 else (oc - 1, NTT - 2 + tt)
                y2q.wait(v, st_y2[pk])
            ev_y2[(oc, tt)] = ev.inc(v.tensor_copy(ysm[tt % 2][:, :], bank[:, :]))
            sy.wait_ge(ev.h, ev_y2[(oc, tt)])
            st_y2[(oc, tt)] = y2q.dma(sy, tt,
                y2d[tt * 128:(tt + 1) * 128, oc * 512:(oc + 1) * 512],
                ysm[tt % 2][:, :])
    fence(nc, c_.sems, alleng)
    sI.close()
    sG.close()

    emit_outln_residual(nc, c_, x1d, y2d, xout, f3x, f3y, xoq, ev, ac, eps_ap,
                        pfx=pfx + "o")
    fence(nc, c_.sems, alleng)


def build_layer(layer):
    c = Ctx()
    nc = c.nc
    xin = nc.dram_tensor("xin", [TPC, D], F32, kind="ExternalInput").ap()
    wq_r = nc.dram_tensor("wq_r", [NDC, 128, D], F32, kind="ExternalInput").ap()
    wk_r = nc.dram_tensor("wk_r", [KD // 128, 128, D], F32, kind="ExternalInput").ap()
    wv_r = nc.dram_tensor("wv_r", [KD // 512, 128, NDC * 512], F32, kind="ExternalInput").ap()
    qcos = nc.dram_tensor("qcos", [128, TPC], F32, kind="ExternalInput").ap()
    qsin = nc.dram_tensor("qsin", [128, TPC], F32, kind="ExternalInput").ap()
    kcos = nc.dram_tensor("kcos", [128, TPC], F32, kind="ExternalInput").ap()
    ksin = nc.dram_tensor("ksin", [128, TPC], F32, kind="ExternalInput").ap()
    wo_r = nc.dram_tensor("wo_r", [8, 128, NDC * 512], F32, kind="ExternalInput").ap()
    maskb = nc.dram_tensor("maskb", [TPC, T], BF16, kind="ExternalInput").ap()
    kvidx = nc.dram_tensor("kvidx", [128, 8], I32, kind="ExternalInput").ap()
    fidx = nc.dram_tensor("fidx", [2, 1], I32, kind="ExternalInput").ap()
    epin = nc.dram_tensor("epin", [2, 16], I32, kind="ExternalInput").ap()
    bw_r = nc.dram_tensor("bw_r", [8, 128, NDC * 512], F32, kind="ExternalInput").ap()
    rt_w = nc.dram_tensor("rt_w", [128, NDC * 8], F32, kind="ExternalInput").ap()
    a2_w = nc.dram_tensor("a2_w", [128, NDC * 128], F32, kind="ExternalInput").ap()
    b2_w = nc.dram_tensor("b2_w", [128, D], F32, kind="ExternalInput").ap()
    if layer == 1:
        bias_w = nc.dram_tensor("bias_w", [1, D], F32, kind="ExternalInput").ap()
        thr_w = nc.dram_tensor("thr_w", [128, NDC], F32, kind="ExternalInput").ap()
        bthr = nc.dram_tensor("bthr", [1, 1], F32, kind="ExternalInput").ap()
    xout = nc.dram_tensor("xout", [TPC, D], F32, kind="ExternalOutput").ap()

    shk = nc.dram_tensor("shk", [2 * KD, TPC], F32, kind="Internal",
                         addr_space="Shared").ap()
    shv = nc.dram_tensor("shv", [T, KD], F32, kind="Internal",
                         addr_space="Shared").ap()
    shflag = nc.dram_tensor("shflag", [4, 16], I32, kind="Internal",
                            addr_space="Shared").ap()
    ctxd = nc.dram_tensor("ctxd", [D, TPC], BF16, kind="Internal").ap()
    y1d = nc.dram_tensor("y1d", [TPC, D], F32, kind="Internal").ap()
    x1d = nc.dram_tensor("x1d", [TPC, D], F32, kind="Internal").ap()
    y2d = nc.dram_tensor("y2d", [TPC, D], F32, kind="Internal").ap()
    ident_d = nc.inline_tensor(np.eye(128, dtype=BF), "ident_c").ap()
    ones_d = nc.inline_tensor(np.ones((1, 128), dtype=BF), "ones_c").ap()
    onesf_d = nc.inline_tensor(np.ones((1, 128), dtype=np.float32), "onesf_c").ap()

    NKC = T // 128

    ident = c.sb("ident", [128, 128], BF16)
    ones_sb = c.sb("ones_sb", [1, 128], BF16)
    onesf_sb = c.sb("onesf_sb", [1, 128], F32)
    astat = c.sb("astat", [128, 8], F32)
    kvix = c.sb("kvix", [128, 8], I32)
    fixs = c.sb("fixs", [2, 1], I32)
    eps_i = c.sb("eps_i", [2, 16], I32)
    fl = c.sb("fl", [1, 64], I32)
    flge = c.sb("flge", [1, 64], I32)
    flcnt = c.sb("flcnt", [1, 1], I32)

    mm = c.sem("mm"); ev = c.sem("ev"); ac = c.sem("ac")
    lds = c.sem("lds")
    pol = c.sem("pol")
    xq = QSem(c, "xq", 2)
    wq2 = QSem(c, "wq2", 2)
    wvq = QSem(c, "wvq", 1)
    scq = QSem(c, "scq", 4)
    fsq = QSem(c, "fsq", 1)
    kvr = QSem(c, "kvr", 3)
    mq = QSem(c, "mq", 2)
    cst = QSem(c, "cst", 2)
    w1q = QSem(c, "w1q", 1)
    y1q = QSem(c, "y1q", 2)
    p3x = QSem(c, "p3x", 2); p3y = QSem(c, "p3y", 2)
    x1q = QSem(c, "x1q", 2)
    bwq = QSem(c, "bwq", 2)
    ldg = c.sem("ldg")
    y2q = QSem(c, "y2q", 2)
    f3x = QSem(c, "f3x", 2); f3y = QSem(c, "f3y", 2)
    xoq = QSem(c, "xoq", 2)
    consts = Consts(c, None)
    c.begin()
    g, v, s, te, sy = nc.gpsimd, nc.vector, nc.scalar, nc.tensor, nc.sync
    consts.ev = ev
    eps_ap = consts.ap(nc, EPS)
    consts.seal(nc)
    alleng = [g, v, s, te, sy]

    lds.inc(sy.dma_start(out=ident[:, :], in_=ident_d), 16)
    lds.inc(sy.dma_start(out=ones_sb[:, :], in_=ones_d), 16)
    lds.inc(sy.dma_start(out=onesf_sb[:, :], in_=onesf_d), 16)
    lds.inc(sy.dma_start(out=kvix[:, :], in_=kvidx), 16)
    lds.inc(sy.dma_start(out=fixs[:, :], in_=fidx), 16)
    lds.inc(sy.dma_start(out=eps_i[:, :], in_=epin), 16)
    const_lds = lds.n
    for e_ in (te, v, s, g):
        e_.wait_ge(lds.h, const_lds)

    ffn_extra = {}
    if layer == 1:
        bias_sb = c.sb("bias_sb", [1, D], BF16)
        thr_sb = c.sb("thr_sb", [128, NDC], BF16)
        bthr_sb = c.sb("bthr_sb", [1, 1], F32)
        bthr_bc = c.sb("bthr_bc", [128, 1], F32)
        ldg.inc(g.dma_start(out=bias_sb[:, :], in_=bias_w), 16)
        ldg.inc(g.dma_start(out=thr_sb[:, :], in_=thr_w), 16)
        ldg.inc(g.dma_start(out=bthr_sb[:, :], in_=bthr), 16)
        te.wait_ge(ldg.h, ldg.n)
        pbc = c.ps("pbc", [128, 1], F32)
        mmb = mm.inc(te.matmul(pbc[:, :], onesf_sb[0:1, :], bthr_sb[0:1, 0:1],
                               start=True, stop=True))
        v.wait_ge(mm.h, mmb)
        evb = ev.inc(v.tensor_copy(bthr_bc[:, :], pbc[:, :]))
        s.wait_ge(ev.h, evb)
        ffn_extra = {"bias_sb": bias_sb, "bthr_bc": bthr_bc, "thr_sb": thr_sb}

    from concourse.bass_core_rust import IndirectOffsetOnAxis
    NKCv = T // 128

    sQT = Sub(nc)
    qt_sb = sQT.sb("qt_sb", [128, NH * TPC], BF16)

    sAB = Sub(nc)
    x2T = sAB.sb("x2T", [128, NDC * TPC], BF16)
    sA = Sub(nc)
    x2T_ev = emit_ln_transpose_phase2(nc, c, sA, xin, x2T, ev, ac, mm, xq,
                                      ident, eps_ap)
    fence(nc, c.sems, alleng)
    if "dbg2" in io:
        env["xoq"].dma(sy, 0, io["dbg2"][0:128, :], x2T[:, 0:512])
    sA.close()

    sB1 = Sub(nc)
    vex = sB1.sb("vex", [128, NTT * KD], BF16)
    wvsl = sB1.sb("b_wvsl", [128, NDC * 512], BF16)
    pmmv = [sB1.ps(f"bv_pmm{i}", [128, 512], F32) for i in range(2)]
    vev = {}
    for oc in range(KD // 512):
        if oc >= 1:
            g.wait_ge(mm.h, mm.n)
        wtok = wvq.dma(g, 0, wvsl[:, :], wv_r[oc])
        wvq.wait(te, wtok)
        if oc == 0:
            te.wait_ge(ev.h, x2T_ev)
        for tt in range(NTT):
            bank = pmmv[tt % 2]
            if (oc, tt) not in ((0, 0), (0, 1)):
                pi = (oc, tt - 2) if tt >= 2 else (oc - 1, NTT - 2 + tt)
                te.wait_ge(ev.h, vev[pi])
            for dc in range(NDC):
                ins = te.matmul(bank[:, :],
                                x2T[:, dc * TPC + tt * 128: dc * TPC + tt * 128 + 128],
                                wvsl[:, dc * 512:(dc + 1) * 512],
                                start=(dc == 0), stop=(dc == NDC - 1))
            mmn = mm.inc(ins)
            v.wait_ge(mm.h, mmn)
            vev[(oc, tt)] = ev.inc(v.tensor_copy(
                vex[:, tt * KD + oc * 512: tt * KD + oc * 512 + 512], bank[:, :]))
    for ci in range(8):
        g.wait_ge(ev.h, vev[(1, ci)])
        scq.idma(g, ci, shv, IndirectOffsetOnAxis(kvix[:, ci:ci + 1], 0),
                 vex[:, ci * KD:(ci + 1) * KD])
    scq.wait_all(g)
    fence(nc, c.sems, alleng)
    sB1.close()

    sB2 = Sub(nc)
    kex = sB2.sb("kex", [128, (KD // 128) * TPC], BF16)
    ktab = sB2.sb("ktab", [128, 2 * TPC], F32)
    wsl = [sB2.sb(f"b_wsl{i}", [128, D], BF16) for i in range(2)]
    rot = sB2.sb("b_rot", [128, TPC], BF16)
    rt1 = sB2.sb("b_rt1", [128, TPC], BF16)
    pmm = [sB2.ps(f"b_pmm{i}", [128, 512], F32) for i in range(2)]
    lds.inc(sy.dma_start(out=ktab[:, 0:TPC], in_=kcos), 16)
    lds.inc(sy.dma_start(out=ktab[:, TPC:2 * TPC], in_=ksin), 16)
    v.wait_ge(lds.h, lds.n)
    ev_ch = {}
    mm_ch = {}
    for oc in range(KD // 128):
        if oc >= 2:
            g.wait_ge(mm.h, mm_ch[oc - 2])
        wtok = wq2.dma(g, oc, wsl[oc % 2][:, :], wk_r[oc])
        wq2.wait(te, wtok)
        for tg in range(2):
            bank = pmm[tg]
            if oc >= 1:
                te.wait_ge(ev.h, ev_ch[(oc - 1, tg)])
            for dc in range(NDC):
                ins = te.matmul(bank[:, :], wsl[oc % 2][:, dc * 128:(dc + 1) * 128],
                                x2T[:, dc * TPC + tg * 512: dc * TPC + tg * 512 + 512],
                                start=(dc == 0), stop=(dc == NDC - 1))
            mm_ch[oc] = mm.inc(ins)
            v.wait_ge(mm.h, mm_ch[oc])
            ev_ch[(oc, tg)] = ev.inc(v.tensor_copy(
                kex[:, oc * TPC + tg * 512: oc * TPC + tg * 512 + 512], bank[:, :]))
        qch = kex[:, oc * TPC:(oc + 1) * TPC]
        v.wait_ge(ev.h, ev_ch[(oc, 1)])
        n0 = ev.inc(v.tensor_copy(rot[0:64, :], qch[64:128, :]))
        n0 = ev.inc(v.tensor_copy(rot[64:128, :], qch[0:64, :]))
        v.wait_ge(ev.h, n0)
        n0 = ev.inc(v.tensor_tensor(rt1[:, :], rot[:, :], ktab[:, TPC:2 * TPC], ALU.mult))
        v.wait_ge(ev.h, n0)
        n0 = ev.inc(v.tensor_tensor(rot[:, :], qch, ktab[:, 0:TPC], ALU.mult))
        v.wait_ge(ev.h, n0)
        ev_ch[(oc, "r")] = ev.inc(v.tensor_tensor(qch, rot[:, :], rt1[:, :], ALU.add))
    for ci in range(8):
        g.wait_ge(ev.h, ev_ch[(ci, "r")])
        scq.idma(g, ci, shk, IndirectOffsetOnAxis(kvix[:, ci:ci + 1], 0),
                 kex[:, ci * TPC:(ci + 1) * TPC])
    scq.wait_all(g)
    fsq.idma(g, 0, shflag, IndirectOffsetOnAxis(fixs[:, :], 0), eps_i[:, :])
    fsq.wait_all(g)
    fence(nc, c.sems, alleng)
    if "dbg2" in io:
        env["xoq"].dma(sy, 1, io["dbg2"][128:256, :], kex[:, 0:512])
    sB2.close()

    sC = Sub(nc)
    qtab = sC.sb("qtab", [128, 2 * TPC], F32)
    cwsl = [sC.sb(f"c_wsl{i}", [128, D], BF16) for i in range(2)]
    crot = sC.sb("c_rot", [128, TPC], BF16)
    crt1 = sC.sb("c_rt1", [128, TPC], BF16)
    cpmm = [sC.ps(f"c_pmm{i}", [128, 512], F32) for i in range(2)]
    lds.inc(sy.dma_start(out=qtab[:, 0:TPC], in_=qcos), 16)
    lds.inc(sy.dma_start(out=qtab[:, TPC:2 * TPC], in_=qsin), 16)
    v.wait_ge(lds.h, lds.n)
    ev_q = {}
    mm_q = {}
    for oc in range(NDC):
        if oc >= 2:
            g.wait_ge(mm.h, mm_q[oc - 2])
        wtok = wq2.dma(g, oc, cwsl[oc % 2][:, :], wq_r[oc])
        wq2.wait(te, wtok)
        for tg in range(2):
            bank = cpmm[tg]
            if oc >= 1:
                te.wait_ge(ev.h, ev_q[(oc - 1, tg)])
            for dc in range(NDC):
                ins = te.matmul(bank[:, :], cwsl[oc % 2][:, dc * 128:(dc + 1) * 128],
                                x2T[:, dc * TPC + tg * 512: dc * TPC + tg * 512 + 512],
                                start=(dc == 0), stop=(dc == NDC - 1))
            mm_q[oc] = mm.inc(ins)
            v.wait_ge(mm.h, mm_q[oc])
            ev_q[(oc, tg)] = ev.inc(v.tensor_copy(
                qt_sb[:, oc * TPC + tg * 512: oc * TPC + tg * 512 + 512], bank[:, :]))
        qch = qt_sb[:, oc * TPC:(oc + 1) * TPC]
        v.wait_ge(ev.h, ev_q[(oc, 1)])
        n0 = ev.inc(v.tensor_copy(crot[0:64, :], qch[64:128, :]))
        n0 = ev.inc(v.tensor_copy(crot[64:128, :], qch[0:64, :]))
        v.wait_ge(ev.h, n0)
        n0 = ev.inc(v.tensor_tensor(crt1[:, :], crot[:, :], qtab[:, TPC:2 * TPC], ALU.mult))
        v.wait_ge(ev.h, n0)
        n0 = ev.inc(v.tensor_tensor(crot[:, :], qch, qtab[:, 0:TPC], ALU.mult))
        v.wait_ge(ev.h, n0)
        ev_q[(oc, "r")] = ev.inc(v.tensor_tensor(qch, crot[:, :], crt1[:, :], ALU.add))
    fence(nc, c.sems, alleng)
    if "dbg2" in io:
        env["xoq"].dma(sy, 0, io["dbg2"][256:384, :], qt_sb[:, 0:512])
    sC.close()
    sAB.close()

    if "dbg_nopoll" not in io:
      with g.register(f"cont{layer}") as cont, g.register(f"w{layer}") as wreg, \
              g.register(f"vreg{layer}") as vreg:
          g.reg_mov(cont, 1)
          g.reg_mov(wreg, 16)
          with g.While(cont):
              g.dma_start(out=fl[:, :],
                          in_=shflag.rearrange("s f -> (s f)")[None, :]).then_inc(pol.h, 16)
              g.wait_ge(pol.h, wreg)
              g.reg_add(wreg, wreg, 16)
              g.tensor_scalar(flge[:, :], fl[:, :], MAGIC + layer, 0, ALU.is_equal, ALU.add)
              with nc.allow_low_precision(reason="int32 flag count"):
                  g.tensor_reduce(flcnt[:, :], flge[:, :], axis=AX.XYZWC, op=ALU.add)
              g.reg_load(vreg, flcnt[0:1, 0:1])
              with g.If_lt(vreg, 64):
                  g.reg_mov(cont, 1)
              with g.Else():
                  g.reg_mov(cont, 0)

    sD = Sub(nc)
    ktf_sb = sD.sb("ktf_sb", [128, NKV * T], BF16)
    vtf_sb = sD.sb("vtf_sb", [128, NKC * KD], BF16)
    msk = [sD.sb(f"msk{i}", [128, T], BF16) for i in range(2)]
    probs = sD.sb("probs", [128, T], BF16)
    probsT = sD.sb("probsT", [128, T], BF16)
    ctxsm = [sD.sb(f"ctxsm{i}", [128, 128], BF16) for i in range(2)]
    pS = sD.ps("pS", [128, T], F32)
    ptr = [sD.ps(f"ptr{i}", [128, 1024], BF16) for i in range(2)]
    pctx = sD.ps("pctx", [128, 128], F32)

    ktf3 = ktf_sb[:, :].rearrange("p (kvh t2) -> p kvh t2", kvh=NKV)
    tks = []
    for hh in range(2):
        tks.append(kvr.dma(g, hh, ktf3[:, :, hh * TPC:(hh + 1) * TPC],
                           shk[hh * KD:(hh + 1) * KD, :].rearrange(
                               "(kvh p) t -> p kvh t", p=128)))
    tks.append(kvr.dma(g, 2, vtf_sb[:, :].rearrange("p (ch k) -> p ch k", ch=NKC),
                       shv.rearrange("(ch p) k -> p ch k", p=128)))
    for tk in tks:
        kvr.wait(te, tk)

    negmax = astat[:, 0:1]; sumexp = astat[:, 1:2]; recip = astat[:, 2:3]
    ac_exp = {}
    mm_tr = {}
    ev_ptc = {}
    st_ctx = {}
    mm_pv = {}
    ld_msk = {}

    def mm_splits(out0, width):
        cuts = []
        s0 = out0
        while s0 < out0 + width:
            e0 = min((s0 // 512 + 1) * 512, out0 + width)
            cuts.append((s0, e0 - s0))
            s0 = e0
        return cuts

    NTv = 1 if "dbg_stop" in io else NTT
    NHv = 1 if "dbg_stop" in io else NH
    for tt in range(NTv):
        lo = 0 if layer == 0 else max(0, tt - 2)
        rng = list(range(lo, tt + 1))
        na = len(rng)
        poslist = [(0, l) for l in rng] + [(1, l) for l in rng]
        nk = len(poslist)
        W = nk * 128
        nblk = (W + 511) // 512
        if tt >= 2:
            sy.wait_ge(mm.h, mm_pv[(tt - 2, NH - 1)])
        ld_msk[tt] = mq.dma(sy, tt, msk[tt % 2][:, :],
                            maskb[tt * 128:(tt + 1) * 128, :])
        for h in range(NHv):
            kvh = h // 4
            mq.wait(te, ld_msk[tt])
            if not (tt == 0 and h == 0):
                te.wait_ge(ac.h, ac.n)
            mm_blk = {}
            for side in range(2):
                out0 = side * na * 128
                ksrc0 = kvh * T + side * TPC + lo * 128
                for (s0, wdt) in mm_splits(out0, na * 128):
                    te.matmul(pS[:, s0:s0 + wdt], ident[:, :],
                              msk[tt % 2][:, s0:s0 + wdt],
                              start=True, stop=False)
                    ins = te.matmul(
                        pS[:, s0:s0 + wdt],
                        qt_sb[:, h * TPC + tt * 128: h * TPC + tt * 128 + 128],
                        ktf_sb[:, ksrc0 + (s0 - out0): ksrc0 + (s0 - out0) + wdt],
                        start=False, stop=True)
                    mm_blk[s0 // 512] = mm.inc(ins)
            nb_used = 0
            for cb in range(nblk):
                wv_ = min(512, W - cb * 512)
                lastmm = max(mv for k2, mv in mm_blk.items() if k2 <= cb)
                v.wait_ge(mm.h, lastmm)
                ev.inc(v.reduce_max(astat[:, 3 + cb:4 + cb],
                                    pS[:, cb * 512:cb * 512 + wv_], axis=AX.X))
                nb_used += 1
            v.wait_ge(ev.h, ev.n)
            evn = ev.inc(v.reduce_max(negmax, astat[:, 3:3 + nb_used], axis=AX.X,
                                      negate=True))
            s.wait_ge(ev.h, evn)
            if not (tt == 0 and h == 0):
                s.wait_ge(mm.h, mm_tr[(tt, h - 1) if h else (tt - 1, NH - 1)])
            ac_exp[(tt, h)] = ac.inc(s.activation(probs[:, 0:W], pS[:, 0:W],
                                                  AF.Exp, bias=negmax, scale=1.0,
                                                  accum_out=sumexp))
            v.wait_ge(ac.h, ac_exp[(tt, h)])
            evr = ev.inc(v.reciprocal(recip, sumexp))
            h1 = (nk // 2) * 128
            evsc = {}
            for gr2, (c0, c1) in enumerate(((0, h1), (h1, W))):
                if c1 <= c0:
                    evsc[gr2] = evr
                    continue
                v.wait_ge(ev.h, evr)
                evsc[gr2] = ev.inc(v.tensor_scalar(
                    probs[:, c0:c1], probs[:, c0:c1], recip, 0.0,
                    ALU.mult, ALU.add))
            ngr = (nk + 7) // 8
            for gr2 in range(ngr):
                bank = ptr[gr2 % 2]
                k0, k1 = gr2 * 8, min(nk, gr2 * 8 + 8)
                te.wait_ge(ev.h, evsc[0] if k0 * 128 < h1 else evsc[1])
                if k1 * 128 > h1:
                    te.wait_ge(ev.h, evsc[1])
                pk = (tt, h, gr2 - 2) if gr2 >= 2 else None
                prev = ev_ptc.get((tt, h, gr2 - 2))
                if prev is None:
                    prev = ev_ptc.get(("last", gr2 % 2))
                if prev is not None:
                    te.wait_ge(ev.h, prev)
                for kc in range(k0, k1):
                    ins = te.transpose(bank[:, (kc - k0) * 128:(kc - k0 + 1) * 128],
                                       probs[:, kc * 128:(kc + 1) * 128], ident[:, :])
                trn = mm.inc(ins)
                v.wait_ge(mm.h, trn)
                ev_ptc[(tt, h, gr2)] = ev.inc(v.tensor_copy(
                    probsT[:, k0 * 128:k1 * 128],
                    bank[:, 0:(k1 - k0) * 128]))
                ev_ptc[("last", gr2 % 2)] = ev_ptc[(tt, h, gr2)]
            mm_tr[(tt, h)] = mm.n
            if tt == 0 and h == 0 and "dbgp" in io:
                mq.wait(sy, ld_msk[tt])
                for tk_ in tks:
                    kvr.wait(sy, tk_)
                env["xoq"].dma(sy, 0, io["dbgst"][0:2, 0:8],
                               env["eps_i"][:, 0:8].bitcast(F32))
                dS = sD.sb("dbg_dS", [128, 512], F32)
                vv2 = ev.inc(v.tensor_copy(dS[:, 0:256], pS[:, 0:256]))
                sy.wait_ge(ev.h, ev.n)
                sy.wait_ge(ac.h, ac.n)
                env["xoq"].dma(sy, 0, io["dbgp"][:, 0:256], probs[:, 0:256])
                env["xoq"].dma(sy, 1, io["dbgpT"][:, 0:256], probsT[:, 0:256])
                env["xoq"].dma(sy, 0, io["dbgst"][2:66, 0:4], astat[0:64, 0:4])
                env["xoq"].dma(sy, 1, io["dbgS"][:, 0:256], dS[:, 0:256])
                env["xoq"].dma(sy, 0, io["dbgin"][0:128, :],
                               msk[0][:, 0:512])
                env["xoq"].dma(sy, 1, io["dbgin"][128:256, 0:128],
                               qt_sb[:, 0:128])
                env["xoq"].dma(sy, 0, io["dbgin"][128:256, 128:256],
                               ktf_sb[:, 0:128])
                env["xoq"].dma(sy, 1, io["dbgin"][128:256, 256:384],
                               vtf_sb[:, 0:128])
            for gr2 in range(ngr):
                te.wait_ge(ev.h, ev_ptc[(tt, h, gr2)])
            for idx, (side, l) in enumerate(poslist):
                vc = side * 8 + l
                ins = te.matmul(pctx[:, :],
                                vtf_sb[:, vc * KD + kvh * 128: vc * KD + kvh * 128 + 128],
                                probsT[:, idx * 128:(idx + 1) * 128],
                                start=(idx == 0), stop=(idx == nk - 1))
            mm_pv[(tt, h)] = mm.inc(ins)
            v.wait_ge(mm.h, mm_pv[(tt, h)])
            if not (tt == 0 and h <= 1):
                pk = (tt, h - 2) if h >= 2 else (tt - 1, NH - 2 + h)
                cst.wait(v, st_ctx[pk])
            evc = ev.inc(v.tensor_copy(ctxsm[h % 2][:, :], pctx[:, :]))
            sy.wait_ge(ev.h, evc)
            st_ctx[(tt, h)] = cst.dma(sy, h,
                ctxd[h * 128:(h + 1) * 128, tt * 128:(tt + 1) * 128],
                ctxsm[h % 2][:, :])
    fence(nc, c.sems, alleng)
    if "dbg_stop" in io:
        env["xoq"].wait_all(sy)
        sD.close()
        sQT.close()
        return
    sD.close()
    sQT.close()

    sE = Sub(nc)
    ctxT = sE.sb("ctxT", [128, NDC * TPC], BF16)
    ewsl = sE.sb("e_wsl", [128, NDC * 512], BF16)
    ysm = [sE.sb(f"e_ysm{i}", [128, 512], F32) for i in range(2)]
    epmm = [sE.ps(f"e_pmm{i}", [128, 512], F32) for i in range(2)]
    tctx = kvr.dma(g, 0, ctxT[:, :].rearrange("p (c t) -> p c t", c=NDC),
                   ctxd.rearrange("(c p) t -> p c t", p=128))
    kvr.wait(te, tctx)
    ev_y1 = {}
    st_y1 = {}
    mm_woc = {}
    for oc in range(8):
        if oc >= 2:
            g.wait_ge(mm.h, mm_woc[oc - 2])
        wtok = w1q.dma(g, oc, ewsl[oc % 2][:, :], wo_r[oc])
        w1q.wait(te, wtok)
        for tt in range(NTT):
            bank = epmm[tt % 2]
            if not (oc == 0 and tt <= 1):
                pk = (oc, tt - 2) if tt >= 2 else (oc - 1, NTT - 2 + tt)
                te.wait_ge(ev.h, ev_y1[pk])
            for dc in range(NDC):
                ins = te.matmul(bank[:, :],
                                ctxT[:, dc * TPC + tt * 128: dc * TPC + tt * 128 + 128],
                                ewsl[oc % 2][:, dc * 512:(dc + 1) * 512],
                                start=(dc == 0), stop=(dc == NDC - 1))
            mmn = mm.inc(ins)
            v.wait_ge(mm.h, mmn)
            if not (oc == 0 and tt <= 1):
                pk = (oc, tt - 2) if tt >= 2 else (oc - 1, NTT - 2 + tt)
                y1q.wait(v, st_y1[pk])
            ev_y1[(oc, tt)] = ev.inc(v.tensor_copy(ysm[tt % 2][:, :], bank[:, :]))
            sy.wait_ge(ev.h, ev_y1[(oc, tt)])
            st_y1[(oc, tt)] = y1q.dma(sy, tt,
                y1d[tt * 128:(tt + 1) * 128, oc * 512:(oc + 1) * 512],
                ysm[tt % 2][:, :])
            if tt == NTT - 1:
                mm_woc[oc] = mm.n
    fence(nc, c.sems, alleng)
    sE.close()

    emit_outln_residual(nc, c, xin, y1d, x1d, p3x, p3y, x1q, ev, ac, eps_ap)
    fence(nc, c.sems, alleng)

    emit_ffn_body(nc, c, layer, x1d, xout, y2d, x2Tname="x4T",
                  rt_w=rt_w, a2_w=a2_w, b2_w=b2_w, bw_r=bw_r,
                  ident=ident, ones_sb=ones_sb, ffn_extra=ffn_extra,
                  mm=mm, ev=ev, ac=ac, ldg=ldg, xq=xq, bwq=bwq, y2q=y2q,
                  f3x=f3x, f3y=f3y, xoq=xoq, eps_ap=eps_ap, alleng=alleng, c_=c)
    xoq.wait_all(sy)
    xoq.wait_all(g)
    return c.finish()



def rope_tables(base, positions):
    import jax
    import jax.numpy as jnp
    with jax.default_device(jax.devices("cpu")[0]):
        inv_freq = 1.0 / base ** (jnp.arange(0, HD, 2, dtype=jnp.float32) / HD)
        t = jnp.asarray(np.asarray(positions), dtype=jnp.float32)
        freqs = t[:, None] * inv_freq[None, :]
        return np.asarray(jnp.cos(freqs)), np.asarray(jnp.sin(freqs))


def feat_tables(base, positions, scale):
    cos, sin = rope_tables(base, positions)
    n = len(positions)
    cosT = np.empty((128, n), np.float32)
    sinT = np.empty((128, n), np.float32)
    cosT[0:64] = cos.T
    cosT[64:128] = cos.T
    sinT[0:64] = -sin.T
    sinT[64:128] = sin.T
    return (cosT * np.float32(scale)).astype(np.float32), \
           (sinT * np.float32(scale)).astype(np.float32)


def modeF_slabs(w, n_oc):
    return np.ascontiguousarray(
        w.reshape(NDC, 128, n_oc, 128).transpose(2, 1, 0, 3).reshape(n_oc, 128, D))


def modeT_slabs(w, width=512):
    O = w.shape[1]
    n = O // width
    return np.ascontiguousarray(
        w.reshape(NDC, 128, n, width).transpose(2, 1, 0, 3).reshape(n, 128, NDC * width))


_PROGS = {}


def _emit_layer_phases(c, nc, layer, env, io):
    (g, v, s, te, sy) = env["eng"]
    alleng = env["eng"]
    mm = env["mm"]; ev = env["ev"]; ac = env["ac"]; lds = env["lds"]
    pol = env["pol"][layer]; ldg = env["ldg"]
    xq = env["xq"]; wq2 = env["wq2"]; wvq = env["wvq"]; scq = env["scq"]
    fsq = env["fsq"]; kvr = env["kvr"]; mq = env["mq"]; cst = env["cst"]
    w1q = env["w1q"]; y1q = env["y1q"]; p3x = env["p3x"]; p3y = env["p3y"]
    x1q = env["x1q"]; bwq = env["bwq"]; y2q = env["y2q"]
    f3x = env["f3x"]; f3y = env["f3y"]; xoq = env["xoq"]
    ident = env["ident"]; ones_sb = env["ones_sb"]; onesf_sb = env["onesf_sb"]
    astat = env["astat"]; kvix = env["kvix"]; fixs = env["fixs"]
    eps_i = env["eps_i"]; fl = env["fl"]; flge = env["flge"]; flcnt = env["flcnt"]
    eps_ap = env["eps_ap"]
    xin = io["xin"]; xout = io["xout"]
    wq_r = io["wq_r"]; wk_r = io["wk_r"]; wv_r = io["wv_r"]; wo_r = io["wo_r"]
    qcos = io["qcos"]; qsin = io["qsin"]; kcos = io["kcos"]; ksin = io["ksin"]
    maskb = io["maskb"]
    bw_r = io["bw_r"]; rt_w = io["rt_w"]; a2_w = io["a2_w"]; b2_w = io["b2_w"]
    shk = io["shk"]; shv = io["shv"]; shflag = io["shflag"]
    ctxd = io["ctxd"]; y1d = io["y1d"]; x1d = io["x1d"]; y2d = io["y2d"]
    NKC = T // 128
    from concourse.bass_core_rust import IndirectOffsetOnAxis

    ffn_extra = {}
    if layer == 1:
        bias_sb = c.sb("bias_sb", [1, D], BF16)
        thr_sb = c.sb("thr_sb", [128, NDC], BF16)
        bthr_sb = c.sb("bthr_sb", [1, 1], F32)
        bthr_bc = c.sb("bthr_bc", [128, 1], F32)
        ldg.inc(g.dma_start(out=bias_sb[:, :], in_=io["bias_w"]), 16)
        ldg.inc(g.dma_start(out=thr_sb[:, :], in_=io["thr_w"]), 16)
        ldg.inc(g.dma_start(out=bthr_sb[:, :], in_=io["bthr"]), 16)
        te.wait_ge(ldg.h, ldg.n)
        pbc = c.ps("pbc", [128, 1], F32)
        mmb = mm.inc(te.matmul(pbc[:, :], onesf_sb[0:1, :], bthr_sb[0:1, 0:1],
                               start=True, stop=True))
        v.wait_ge(mm.h, mmb)
        evb = ev.inc(v.tensor_copy(bthr_bc[:, :], pbc[:, :]))
        s.wait_ge(ev.h, evb)
        ffn_extra = {"bias_sb": bias_sb, "bthr_bc": bthr_bc, "thr_sb": thr_sb}

    sQT = Sub(nc)
    qt_sb = sQT.sb(f"l{layer}_qt_sb", [128, NH * TPC], BF16)

    sAB = Sub(nc)
    x2T = sAB.sb(f"l{layer}_x2T", [128, NDC * TPC], BF16)
    sA = Sub(nc)
    x2T_ev = emit_ln_transpose_phase2(nc, c, sA, xin, x2T, ev, ac, mm, xq,
                                      ident, eps_ap, pfx=f"l{layer}_lnA")
    fence(nc, c.sems, alleng)
    if "dbg2" in io:
        env["xoq"].dma(sy, 0, io["dbg2"][0:128, :], x2T[:, 0:512])
    sA.close()

    sB1 = Sub(nc)
    vex = sB1.sb(f"l{layer}_vex", [128, NTT * KD], BF16)
    wvsl = sB1.sb(f"l{layer}_b_wvsl", [128, NDC * 512], BF16)
    pmmv = [sB1.ps(f"l{layer}_bv_pmm{i}", [128, 512], F32) for i in range(2)]
    vev = {}
    for oc in range(KD // 512):
        if oc >= 1:
            g.wait_ge(mm.h, mm.n)
        wtok = wvq.dma(g, 0, wvsl[:, :], wv_r[oc])
        wvq.wait(te, wtok)
        if oc == 0:
            te.wait_ge(ev.h, x2T_ev)
        for tt in range(NTT):
            bank = pmmv[tt % 2]
            if (oc, tt) not in ((0, 0), (0, 1)):
                pi = (oc, tt - 2) if tt >= 2 else (oc - 1, NTT - 2 + tt)
                te.wait_ge(ev.h, vev[pi])
            for dc in range(NDC):
                ins = te.matmul(bank[:, :],
                                x2T[:, dc * TPC + tt * 128: dc * TPC + tt * 128 + 128],
                                wvsl[:, dc * 512:(dc + 1) * 512],
                                start=(dc == 0), stop=(dc == NDC - 1))
            mmn = mm.inc(ins)
            v.wait_ge(mm.h, mmn)
            vev[(oc, tt)] = ev.inc(v.tensor_copy(
                vex[:, tt * KD + oc * 512: tt * KD + oc * 512 + 512], bank[:, :]))
    for ci in range(8):
        g.wait_ge(ev.h, vev[(1, ci)])
        scq.idma(g, ci, shv, IndirectOffsetOnAxis(kvix[:, ci:ci + 1], 0),
                 vex[:, ci * KD:(ci + 1) * KD])
    scq.wait_all(g)
    fence(nc, c.sems, alleng)
    sB1.close()

    sB2 = Sub(nc)
    kex = sB2.sb(f"l{layer}_kex", [128, (KD // 128) * TPC], BF16)
    ktab = sB2.sb(f"l{layer}_ktab", [128, 2 * TPC], F32)
    wsl = [sB2.sb(f"l{layer}_b_wsl{i}", [128, D], BF16) for i in range(2)]
    rot = sB2.sb(f"l{layer}_b_rot", [128, TPC], BF16)
    rt1 = sB2.sb(f"l{layer}_b_rt1", [128, TPC], BF16)
    pmm = [sB2.ps(f"l{layer}_b_pmm{i}", [128, 512], F32) for i in range(2)]
    lds.inc(sy.dma_start(out=ktab[:, 0:TPC], in_=kcos), 16)
    lds.inc(sy.dma_start(out=ktab[:, TPC:2 * TPC], in_=ksin), 16)
    v.wait_ge(lds.h, lds.n)
    ev_ch = {}
    mm_ch = {}
    for oc in range(KD // 128):
        if oc >= 2:
            g.wait_ge(mm.h, mm_ch[oc - 2])
        wtok = wq2.dma(g, oc, wsl[oc % 2][:, :], wk_r[oc])
        wq2.wait(te, wtok)
        for tg in range(2):
            bank = pmm[tg]
            if oc >= 1:
                te.wait_ge(ev.h, ev_ch[(oc - 1, tg)])
            for dc in range(NDC):
                ins = te.matmul(bank[:, :], wsl[oc % 2][:, dc * 128:(dc + 1) * 128],
                                x2T[:, dc * TPC + tg * 512: dc * TPC + tg * 512 + 512],
                                start=(dc == 0), stop=(dc == NDC - 1))
            mm_ch[oc] = mm.inc(ins)
            v.wait_ge(mm.h, mm_ch[oc])
            ev_ch[(oc, tg)] = ev.inc(v.tensor_copy(
                kex[:, oc * TPC + tg * 512: oc * TPC + tg * 512 + 512], bank[:, :]))
        qch = kex[:, oc * TPC:(oc + 1) * TPC]
        v.wait_ge(ev.h, ev_ch[(oc, 1)])
        n0 = ev.inc(v.tensor_copy(rot[0:64, :], qch[64:128, :]))
        n0 = ev.inc(v.tensor_copy(rot[64:128, :], qch[0:64, :]))
        v.wait_ge(ev.h, n0)
        n0 = ev.inc(v.tensor_tensor(rt1[:, :], rot[:, :], ktab[:, TPC:2 * TPC], ALU.mult))
        v.wait_ge(ev.h, n0)
        n0 = ev.inc(v.tensor_tensor(rot[:, :], qch, ktab[:, 0:TPC], ALU.mult))
        v.wait_ge(ev.h, n0)
        ev_ch[(oc, "r")] = ev.inc(v.tensor_tensor(qch, rot[:, :], rt1[:, :], ALU.add))
    for ci in range(8):
        g.wait_ge(ev.h, ev_ch[(ci, "r")])
        scq.idma(g, ci, shk, IndirectOffsetOnAxis(kvix[:, ci:ci + 1], 0),
                 kex[:, ci * TPC:(ci + 1) * TPC])
    scq.wait_all(g)
    fsq.idma(g, 0, shflag, IndirectOffsetOnAxis(fixs[:, :], 0), eps_i[:, :])
    fsq.wait_all(g)
    fence(nc, c.sems, alleng)
    if "dbg2" in io:
        env["xoq"].dma(sy, 1, io["dbg2"][128:256, :], kex[:, 0:512])
    sB2.close()

    sC = Sub(nc)
    qtab = sC.sb(f"l{layer}_qtab", [128, 2 * TPC], F32)
    cwsl = [sC.sb(f"l{layer}_c_wsl{i}", [128, D], BF16) for i in range(2)]
    crot = sC.sb(f"l{layer}_c_rot", [128, TPC], BF16)
    crt1 = sC.sb(f"l{layer}_c_rt1", [128, TPC], BF16)
    cpmm = [sC.ps(f"l{layer}_c_pmm{i}", [128, 512], F32) for i in range(2)]
    lds.inc(sy.dma_start(out=qtab[:, 0:TPC], in_=qcos), 16)
    lds.inc(sy.dma_start(out=qtab[:, TPC:2 * TPC], in_=qsin), 16)
    v.wait_ge(lds.h, lds.n)
    ev_q = {}
    mm_q = {}
    for oc in range(NDC):
        if oc >= 2:
            g.wait_ge(mm.h, mm_q[oc - 2])
        wtok = wq2.dma(g, oc, cwsl[oc % 2][:, :], wq_r[oc])
        wq2.wait(te, wtok)
        for tg in range(2):
            bank = cpmm[tg]
            if oc >= 1:
                te.wait_ge(ev.h, ev_q[(oc - 1, tg)])
            for dc in range(NDC):
                ins = te.matmul(bank[:, :], cwsl[oc % 2][:, dc * 128:(dc + 1) * 128],
                                x2T[:, dc * TPC + tg * 512: dc * TPC + tg * 512 + 512],
                                start=(dc == 0), stop=(dc == NDC - 1))
            mm_q[oc] = mm.inc(ins)
            v.wait_ge(mm.h, mm_q[oc])
            ev_q[(oc, tg)] = ev.inc(v.tensor_copy(
                qt_sb[:, oc * TPC + tg * 512: oc * TPC + tg * 512 + 512], bank[:, :]))
        qch = qt_sb[:, oc * TPC:(oc + 1) * TPC]
        v.wait_ge(ev.h, ev_q[(oc, 1)])
        n0 = ev.inc(v.tensor_copy(crot[0:64, :], qch[64:128, :]))
        n0 = ev.inc(v.tensor_copy(crot[64:128, :], qch[0:64, :]))
        v.wait_ge(ev.h, n0)
        n0 = ev.inc(v.tensor_tensor(crt1[:, :], crot[:, :], qtab[:, TPC:2 * TPC], ALU.mult))
        v.wait_ge(ev.h, n0)
        n0 = ev.inc(v.tensor_tensor(crot[:, :], qch, qtab[:, 0:TPC], ALU.mult))
        v.wait_ge(ev.h, n0)
        ev_q[(oc, "r")] = ev.inc(v.tensor_tensor(qch, crot[:, :], crt1[:, :], ALU.add))
    fence(nc, c.sems, alleng)
    if "dbg2" in io:
        env["xoq"].dma(sy, 0, io["dbg2"][256:384, :], qt_sb[:, 0:512])
    sC.close()
    sAB.close()

    if "dbg_nopoll" not in io:
      with g.register(f"cont{layer}") as cont, g.register(f"w{layer}") as wreg, \
              g.register(f"vreg{layer}") as vreg:
          g.reg_mov(cont, 1)
          g.reg_mov(wreg, 16)
          with g.While(cont):
              g.dma_start(out=fl[:, :],
                          in_=shflag.rearrange("s f -> (s f)")[None, :]).then_inc(pol.h, 16)
              g.wait_ge(pol.h, wreg)
              g.reg_add(wreg, wreg, 16)
              g.tensor_scalar(flge[:, :], fl[:, :], MAGIC, 0, ALU.is_equal, ALU.add)
              with nc.allow_low_precision(reason="int32 flag count"):
                  g.tensor_reduce(flcnt[:, :], flge[:, :], axis=AX.XYZWC, op=ALU.add)
              g.reg_load(vreg, flcnt[0:1, 0:1])
              with g.If_lt(vreg, 64):
                  g.reg_mov(cont, 1)
              with g.Else():
                  g.reg_mov(cont, 0)

    sD = Sub(nc)
    ktf_sb = sD.sb(f"l{layer}_ktf_sb", [128, NKV * T], BF16)
    vtf_sb = sD.sb(f"l{layer}_vtf_sb", [128, NKC * KD], BF16)
    msk = [sD.sb(f"l{layer}_msk{i}", [128, T], BF16) for i in range(2)]
    probs = sD.sb(f"l{layer}_probs", [128, T], BF16)
    probsT = sD.sb(f"l{layer}_probsT", [128, T], BF16)
    ctxsm = [sD.sb(f"l{layer}_ctxsm{i}", [128, 128], BF16) for i in range(2)]
    pS = sD.ps(f"l{layer}_pS", [128, T], F32)
    ptr = [sD.ps(f"l{layer}_ptr{i}", [128, 1024], BF16) for i in range(2)]
    pctx = sD.ps(f"l{layer}_pctx", [128, 128], F32)

    ktf3 = ktf_sb[:, :].rearrange("p (kvh t2) -> p kvh t2", kvh=NKV)
    tks = []
    for hh in range(2):
        tks.append(kvr.dma(g, hh, ktf3[:, :, hh * TPC:(hh + 1) * TPC],
                           shk[hh * KD:(hh + 1) * KD, :].rearrange(
                               "(kvh p) t -> p kvh t", p=128)))
    tks.append(kvr.dma(g, 2, vtf_sb[:, :].rearrange("p (ch k) -> p ch k", ch=NKC),
                       shv.rearrange("(ch p) k -> p ch k", p=128)))
    for tk in tks:
        kvr.wait(te, tk)

    negmax = astat[:, 0:1]; sumexp = astat[:, 1:2]; recip = astat[:, 2:3]
    ac_exp = {}
    mm_tr = {}
    ev_ptc = {}
    st_ctx = {}
    mm_pv = {}
    ld_msk = {}

    def mm_splits(out0, width):
        cuts = []
        s0 = out0
        while s0 < out0 + width:
            e0 = min((s0 // 512 + 1) * 512, out0 + width)
            cuts.append((s0, e0 - s0))
            s0 = e0
        return cuts

    NTv = 1 if "dbg_stop" in io else NTT
    NHv = 1 if "dbg_stop" in io else NH
    for tt in range(NTv):
        lo = 0 if layer == 0 else max(0, tt - 2)
        rng = list(range(lo, tt + 1))
        na = len(rng)
        poslist = [(0, l) for l in rng] + [(1, l) for l in rng]
        nk = len(poslist)
        W = nk * 128
        nblk = (W + 511) // 512
        if tt >= 2:
            sy.wait_ge(mm.h, mm_pv[(tt - 2, NH - 1)])
        ld_msk[tt] = mq.dma(sy, tt, msk[tt % 2][:, :],
                            maskb[tt * 128:(tt + 1) * 128, :])
        for h in range(NHv):
            kvh = h // 4
            mq.wait(te, ld_msk[tt])
            if not (tt == 0 and h == 0):
                te.wait_ge(ac.h, ac.n)
            mm_blk = {}
            for side in range(2):
                out0 = side * na * 128
                ksrc0 = kvh * T + side * TPC + lo * 128
                for (s0, wdt) in mm_splits(out0, na * 128):
                    te.matmul(pS[:, s0:s0 + wdt], ident[:, :],
                              msk[tt % 2][:, s0:s0 + wdt],
                              start=True, stop=False)
                    ins = te.matmul(
                        pS[:, s0:s0 + wdt],
                        qt_sb[:, h * TPC + tt * 128: h * TPC + tt * 128 + 128],
                        ktf_sb[:, ksrc0 + (s0 - out0): ksrc0 + (s0 - out0) + wdt],
                        start=False, stop=True)
                    mm_blk[s0 // 512] = mm.inc(ins)
            nb_used = 0
            for cb in range(nblk):
                wv_ = min(512, W - cb * 512)
                lastmm = max(mv for k2, mv in mm_blk.items() if k2 <= cb)
                v.wait_ge(mm.h, lastmm)
                ev.inc(v.reduce_max(astat[:, 3 + cb:4 + cb],
                                    pS[:, cb * 512:cb * 512 + wv_], axis=AX.X))
                nb_used += 1
            v.wait_ge(ev.h, ev.n)
            evn = ev.inc(v.reduce_max(negmax, astat[:, 3:3 + nb_used], axis=AX.X,
                                      negate=True))
            s.wait_ge(ev.h, evn)
            if not (tt == 0 and h == 0):
                s.wait_ge(mm.h, mm_tr[(tt, h - 1) if h else (tt - 1, NH - 1)])
            ac_exp[(tt, h)] = ac.inc(s.activation(probs[:, 0:W], pS[:, 0:W],
                                                  AF.Exp, bias=negmax, scale=1.0,
                                                  accum_out=sumexp))
            v.wait_ge(ac.h, ac_exp[(tt, h)])
            evr = ev.inc(v.reciprocal(recip, sumexp))
            h1 = (nk // 2) * 128
            evsc = {}
            for gr2, (c0, c1) in enumerate(((0, h1), (h1, W))):
                if c1 <= c0:
                    evsc[gr2] = evr
                    continue
                v.wait_ge(ev.h, evr)
                evsc[gr2] = ev.inc(v.tensor_scalar(
                    probs[:, c0:c1], probs[:, c0:c1], recip, 0.0,
                    ALU.mult, ALU.add))
            ngr = (nk + 7) // 8
            for gr2 in range(ngr):
                bank = ptr[gr2 % 2]
                k0, k1 = gr2 * 8, min(nk, gr2 * 8 + 8)
                te.wait_ge(ev.h, evsc[0] if k0 * 128 < h1 else evsc[1])
                if k1 * 128 > h1:
                    te.wait_ge(ev.h, evsc[1])
                pk = (tt, h, gr2 - 2) if gr2 >= 2 else None
                prev = ev_ptc.get((tt, h, gr2 - 2))
                if prev is None:
                    prev = ev_ptc.get(("last", gr2 % 2))
                if prev is not None:
                    te.wait_ge(ev.h, prev)
                for kc in range(k0, k1):
                    ins = te.transpose(bank[:, (kc - k0) * 128:(kc - k0 + 1) * 128],
                                       probs[:, kc * 128:(kc + 1) * 128], ident[:, :])
                trn = mm.inc(ins)
                v.wait_ge(mm.h, trn)
                ev_ptc[(tt, h, gr2)] = ev.inc(v.tensor_copy(
                    probsT[:, k0 * 128:k1 * 128],
                    bank[:, 0:(k1 - k0) * 128]))
                ev_ptc[("last", gr2 % 2)] = ev_ptc[(tt, h, gr2)]
            mm_tr[(tt, h)] = mm.n
            if tt == 0 and h == 0 and "dbgp" in io:
                mq.wait(sy, ld_msk[tt])
                for tk_ in tks:
                    kvr.wait(sy, tk_)
                env["xoq"].dma(sy, 0, io["dbgst"][0:2, 0:8],
                               env["eps_i"][:, 0:8].bitcast(F32))
                dS = sD.sb("dbg_dS", [128, 512], F32)
                vv2 = ev.inc(v.tensor_copy(dS[:, 0:256], pS[:, 0:256]))
                sy.wait_ge(ev.h, ev.n)
                sy.wait_ge(ac.h, ac.n)
                env["xoq"].dma(sy, 0, io["dbgp"][:, 0:256], probs[:, 0:256])
                env["xoq"].dma(sy, 1, io["dbgpT"][:, 0:256], probsT[:, 0:256])
                env["xoq"].dma(sy, 0, io["dbgst"][2:66, 0:4], astat[0:64, 0:4])
                env["xoq"].dma(sy, 1, io["dbgS"][:, 0:256], dS[:, 0:256])
                env["xoq"].dma(sy, 0, io["dbgin"][0:128, :],
                               msk[0][:, 0:512])
                env["xoq"].dma(sy, 1, io["dbgin"][128:256, 0:128],
                               qt_sb[:, 0:128])
                env["xoq"].dma(sy, 0, io["dbgin"][128:256, 128:256],
                               ktf_sb[:, 0:128])
                env["xoq"].dma(sy, 1, io["dbgin"][128:256, 256:384],
                               vtf_sb[:, 0:128])
            for gr2 in range(ngr):
                te.wait_ge(ev.h, ev_ptc[(tt, h, gr2)])
            for idx, (side, l) in enumerate(poslist):
                vc = side * 8 + l
                ins = te.matmul(pctx[:, :],
                                vtf_sb[:, vc * KD + kvh * 128: vc * KD + kvh * 128 + 128],
                                probsT[:, idx * 128:(idx + 1) * 128],
                                start=(idx == 0), stop=(idx == nk - 1))
            mm_pv[(tt, h)] = mm.inc(ins)
            v.wait_ge(mm.h, mm_pv[(tt, h)])
            if not (tt == 0 and h <= 1):
                pk = (tt, h - 2) if h >= 2 else (tt - 1, NH - 2 + h)
                cst.wait(v, st_ctx[pk])
            evc = ev.inc(v.tensor_copy(ctxsm[h % 2][:, :], pctx[:, :]))
            sy.wait_ge(ev.h, evc)
            st_ctx[(tt, h)] = cst.dma(sy, h,
                ctxd[h * 128:(h + 1) * 128, tt * 128:(tt + 1) * 128],
                ctxsm[h % 2][:, :])
    fence(nc, c.sems, alleng)
    if "dbg_stop" in io:
        env["xoq"].wait_all(sy)
        sD.close()
        sQT.close()
        return
    sD.close()
    sQT.close()

    sE = Sub(nc)
    ctxT = sE.sb(f"l{layer}_ctxT", [128, NDC * TPC], BF16)
    ewsl = [sE.sb(f"l{layer}_e_wsl{i}", [128, NDC * 512], BF16) for i in range(2)]
    ysm = [sE.sb(f"l{layer}_e_ysm{i}", [128, 512], F32) for i in range(2)]
    epmm = [sE.ps(f"l{layer}_e_pmm{i}", [128, 512], F32) for i in range(2)]
    tctx = kvr.dma(g, 0, ctxT[:, :].rearrange("p (c t) -> p c t", c=NDC),
                   ctxd.rearrange("(c p) t -> p c t", p=128))
    kvr.wait(te, tctx)
    ev_y1 = {}
    st_y1 = {}
    mm_woc = {}
    for oc in range(8):
        if oc >= 2:
            g.wait_ge(mm.h, mm_woc[oc - 2])
        wtok = w1q.dma(g, oc, ewsl[oc % 2][:, :], wo_r[oc])
        w1q.wait(te, wtok)
        for tt in range(NTT):
            bank = epmm[tt % 2]
            if not (oc == 0 and tt <= 1):
                pk = (oc, tt - 2) if tt >= 2 else (oc - 1, NTT - 2 + tt)
                te.wait_ge(ev.h, ev_y1[pk])
            for dc in range(NDC):
                ins = te.matmul(bank[:, :],
                                ctxT[:, dc * TPC + tt * 128: dc * TPC + tt * 128 + 128],
                                ewsl[oc % 2][:, dc * 512:(dc + 1) * 512],
                                start=(dc == 0), stop=(dc == NDC - 1))
            mmn = mm.inc(ins)
            v.wait_ge(mm.h, mmn)
            if not (oc == 0 and tt <= 1):
                pk = (oc, tt - 2) if tt >= 2 else (oc - 1, NTT - 2 + tt)
                y1q.wait(v, st_y1[pk])
            ev_y1[(oc, tt)] = ev.inc(v.tensor_copy(ysm[tt % 2][:, :], bank[:, :]))
            sy.wait_ge(ev.h, ev_y1[(oc, tt)])
            st_y1[(oc, tt)] = y1q.dma(sy, tt,
                y1d[tt * 128:(tt + 1) * 128, oc * 512:(oc + 1) * 512],
                ysm[tt % 2][:, :])
            if tt == NTT - 1:
                mm_woc[oc] = mm.n
    fence(nc, c.sems, alleng)
    sE.close()

    emit_outln_residual(nc, c, xin, y1d, x1d, p3x, p3y, x1q, ev, ac, eps_ap,
                        pfx=f"l{layer}_o")
    fence(nc, c.sems, alleng)

    emit_ffn_body(nc, c, layer, x1d, xout, y2d, x2Tname=f"l{layer}_x4T",
                  rt_w=rt_w, a2_w=a2_w, b2_w=b2_w, bw_r=bw_r,
                  ident=ident, ones_sb=ones_sb, ffn_extra=ffn_extra,
                  mm=mm, ev=ev, ac=ac, ldg=ldg, xq=xq, bwq=bwq, y2q=y2q,
                  f3x=f3x, f3y=f3y, xoq=xoq, eps_ap=eps_ap, alleng=alleng, c_=c,
                  pfx=f"l{layer}_f")




def rope_tables(base, positions):
    import jax
    import jax.numpy as jnp
    with jax.default_device(jax.devices("cpu")[0]):
        inv_freq = 1.0 / base ** (jnp.arange(0, HD, 2, dtype=jnp.float32) / HD)
        t = jnp.asarray(np.asarray(positions), dtype=jnp.float32)
        freqs = t[:, None] * inv_freq[None, :]
        return np.asarray(jnp.cos(freqs)), np.asarray(jnp.sin(freqs))


def feat_tables(base, positions, scale):
    cos, sin = rope_tables(base, positions)
    n = len(positions)
    cosT = np.empty((128, n), np.float32)
    sinT = np.empty((128, n), np.float32)
    cosT[0:64] = cos.T
    cosT[64:128] = cos.T
    sinT[0:64] = -sin.T
    sinT[64:128] = sin.T
    return (cosT * np.float32(scale)).astype(np.float32), \
           (sinT * np.float32(scale)).astype(np.float32)


def modeF_slabs(w, n_oc):
    return np.ascontiguousarray(
        w.reshape(NDC, 128, n_oc, 128).transpose(2, 1, 0, 3).reshape(n_oc, 128, D))


def modeT_slabs(w, width=512):
    O = w.shape[1]
    n = O // width
    return np.ascontiguousarray(
        w.reshape(NDC, 128, n, width).transpose(2, 1, 0, 3).reshape(n, 128, NDC * width))


_PROGS = {}




def build_full():
    c = Ctx()
    nc = c.nc
    ident_d = nc.inline_tensor(np.eye(128, dtype=BF), "ident_c").ap()
    ones_d = nc.inline_tensor(np.ones((1, 128), dtype=BF), "ones_c").ap()
    onesf_d = nc.inline_tensor(np.ones((1, 128), dtype=np.float32), "onesf_c").ap()
    kvidx = nc.dram_tensor("kvidx", [128, 8], I32, kind="ExternalInput").ap()
    fidx = nc.dram_tensor("fidx", [2, 1], I32, kind="ExternalInput").ap()
    epin = nc.dram_tensor("epin", [2, 16], I32, kind="ExternalInput").ap()
    xmid = nc.dram_tensor("xmid", [TPC, D], F32, kind="Internal").ap()
    ios = []
    for L in (0, 1):
        io = {}
        io["xin"] = (nc.dram_tensor("xin", [TPC, D], F32, kind="ExternalInput").ap()
                     if L == 0 else xmid)
        io["xout"] = (xmid if L == 0 else
                      nc.dram_tensor("xout", [TPC, D], F32, kind="ExternalOutput").ap())
        io["wq_r"] = nc.dram_tensor(f"wq_r{L}", [NDC, 128, D], F32, kind="ExternalInput").ap()
        io["wk_r"] = nc.dram_tensor(f"wk_r{L}", [KD // 128, 128, D], F32, kind="ExternalInput").ap()
        io["wv_r"] = nc.dram_tensor(f"wv_r{L}", [KD // 512, 128, NDC * 512], F32, kind="ExternalInput").ap()
        io["qcos"] = nc.dram_tensor(f"qcos{L}", [128, TPC], F32, kind="ExternalInput").ap()
        io["qsin"] = nc.dram_tensor(f"qsin{L}", [128, TPC], F32, kind="ExternalInput").ap()
        io["kcos"] = nc.dram_tensor(f"kcos{L}", [128, TPC], F32, kind="ExternalInput").ap()
        io["ksin"] = nc.dram_tensor(f"ksin{L}", [128, TPC], F32, kind="ExternalInput").ap()
        io["wo_r"] = nc.dram_tensor(f"wo_r{L}", [8, 128, NDC * 512], F32, kind="ExternalInput").ap()
        io["maskb"] = nc.dram_tensor(f"maskb{L}", [TPC, T], BF16, kind="ExternalInput").ap()
        io["bw_r"] = nc.dram_tensor(f"bw_r{L}", [8, 128, NDC * 512], F32, kind="ExternalInput").ap()
        io["rt_w"] = nc.dram_tensor(f"rt_w{L}", [128, NDC * 8], F32, kind="ExternalInput").ap()
        io["a2_w"] = nc.dram_tensor(f"a2_w{L}", [128, NDC * 128], F32, kind="ExternalInput").ap()
        io["b2_w"] = nc.dram_tensor(f"b2_w{L}", [128, D], F32, kind="ExternalInput").ap()
        if L == 1:
            io["bias_w"] = nc.dram_tensor("bias_w", [1, D], F32, kind="ExternalInput").ap()
            io["thr_w"] = nc.dram_tensor("thr_w", [128, NDC], F32, kind="ExternalInput").ap()
            io["bthr"] = nc.dram_tensor("bthr", [1, 1], F32, kind="ExternalInput").ap()
        io["shk"] = nc.dram_tensor(f"shk{L}", [2 * KD, TPC], F32, kind="Internal",
                                   addr_space="Shared").ap()
        io["shv"] = nc.dram_tensor(f"shv{L}", [T, KD], F32, kind="Internal",
                                   addr_space="Shared").ap()
        io["shflag"] = nc.dram_tensor(f"shflag{L}", [4, 16], I32, kind="Internal",
                                      addr_space="Shared").ap()
        io["ctxd"] = nc.dram_tensor(f"ctxd{L}", [D, TPC], BF16, kind="Internal").ap()
        io["y1d"] = nc.dram_tensor(f"y1d{L}", [TPC, D], F32, kind="Internal").ap()
        io["x1d"] = nc.dram_tensor(f"x1d{L}", [TPC, D], F32, kind="Internal").ap()
        io["y2d"] = nc.dram_tensor(f"y2d{L}", [TPC, D], F32, kind="Internal").ap()
        ios.append(io)

    ident = c.sb("ident", [128, 128], BF16)
    ones_sb = c.sb("ones_sb", [1, 128], BF16)
    onesf_sb = c.sb("onesf_sb", [1, 128], F32)
    astat = c.sb("astat", [128, 8], F32)
    kvix = c.sb("kvix", [128, 8], I32)
    fixs = c.sb("fixs", [2, 1], I32)
    eps_i = c.sb("eps_i", [2, 16], I32)
    fl = c.sb("fl", [1, 64], I32)
    flge = c.sb("flge", [1, 64], I32)
    flcnt = c.sb("flcnt", [1, 1], I32)
    env = {}
    env["mm"] = c.sem("mm"); env["ev"] = c.sem("ev"); env["ac"] = c.sem("ac")
    env["lds"] = c.sem("lds"); env["ldg"] = c.sem("ldg")
    env["pol"] = [c.sem("pol0"), c.sem("pol1")]
    env["xq"] = QSem(c, "xq", 2)
    env["wq2"] = QSem(c, "wq2", 2)
    env["wvq"] = QSem(c, "wvq", 1)
    env["scq"] = QSem(c, "scq", 4)
    env["fsq"] = QSem(c, "fsq", 1)
    env["kvr"] = QSem(c, "kvr", 3)
    env["mq"] = QSem(c, "mq", 2)
    env["cst"] = QSem(c, "cst", 2)
    env["w1q"] = QSem(c, "w1q", 2)
    env["y1q"] = QSem(c, "y1q", 2)
    env["p3x"] = QSem(c, "p3x", 2); env["p3y"] = QSem(c, "p3y", 2)
    env["x1q"] = QSem(c, "x1q", 2)
    env["bwq"] = QSem(c, "bwq", 2)
    env["y2q"] = QSem(c, "y2q", 2)
    env["f3x"] = QSem(c, "f3x", 2); env["f3y"] = QSem(c, "f3y", 2)
    env["xoq"] = QSem(c, "xoq", 2)
    consts = Consts(c, None)
    c.begin()
    g, v, s, te, sy = nc.gpsimd, nc.vector, nc.scalar, nc.tensor, nc.sync
    env["eng"] = [g, v, s, te, sy]
    consts.ev = env["ev"]
    env["eps_ap"] = consts.ap(nc, EPS)
    consts.seal(nc)
    lds = env["lds"]
    lds.inc(sy.dma_start(out=ident[:, :], in_=ident_d), 16)
    lds.inc(sy.dma_start(out=ones_sb[:, :], in_=ones_d), 16)
    lds.inc(sy.dma_start(out=onesf_sb[:, :], in_=onesf_d), 16)
    lds.inc(sy.dma_start(out=kvix[:, :], in_=kvidx), 16)
    lds.inc(sy.dma_start(out=fixs[:, :], in_=fidx), 16)
    lds.inc(sy.dma_start(out=eps_i[:, :], in_=epin), 16)
    for e_ in (te, v, s, g):
        e_.wait_ge(lds.h, lds.n)
    env.update(ident=ident, ones_sb=ones_sb, onesf_sb=onesf_sb, astat=astat,
               kvix=kvix, fixs=fixs, eps_i=eps_i, fl=fl, flge=flge, flcnt=flcnt)

    for L in (0, 1):
        _emit_layer_phases(c, nc, L, env, ios[L])
    env["xoq"].wait_all(sy)
    env["xoq"].wait_all(g)
    return c.finish()

def _prog(name, fn):
    if name not in _PROGS:
        _PROGS[name] = fn()
    return _PROGS[name]


def _chunks_of(core):
    return [2 * j + (core % 2) for j in range(NTT)]


def _pos_of(core):
    return np.concatenate([np.arange(ch * 128, (ch + 1) * 128)
                           for ch in _chunks_of(core)])


def _mask_packed(core, layer):
    p = core % 2
    m = np.zeros((TPC, T), np.float32)
    for j in range(NTT):
        lo = 0 if layer == 0 else max(0, j - 2)
        rng = list(range(lo, j + 1))
        poslist = [(0, l) for l in rng] + [(1, l) for l in rng]
        gq = (2 * j + p) * 128 + np.arange(128)[:, None]
        for idx, (side, l) in enumerate(poslist):
            gk = (2 * l + side) * 128 + np.arange(128)[None, :]
            ok = gk <= gq
            if layer == 1:
                ok &= (gq - gk) <= 512
            m[j * 128:(j + 1) * 128, idx * 128:(idx + 1) * 128] = \
                np.where(ok, np.float32(0.0), np.float32(NEGBIG))
    return m.astype(BF)


def _mask_for(core, layer):
    half = core % 2
    gq = half * TPC + np.arange(TPC)[:, None]
    j = np.arange(T)[None, :]
    ok = j <= gq
    if layer == 1:
        ok &= (gq - j) <= 512
    return np.where(ok, np.float32(0.0), np.float32(NEGBIG)).astype(BF)


def _ffn_host(inputs, L):
    if L == 0:
        bw = np.ascontiguousarray(inputs["L0_ffn_base_W"].T)
        rt = np.ascontiguousarray(inputs["L0_ffn_router_W"].T)
        A2 = np.ascontiguousarray(
            inputs["L0_ffn_lora_A"].transpose(1, 0, 2).reshape(D, ER))
        B2 = np.ascontiguousarray(inputs["L0_ffn_lora_B"].reshape(ER, D))
    else:
        bw = np.ascontiguousarray(inputs["L1_ffn_W_base"].T)
        rt = np.ascontiguousarray(inputs["L1_ffn_W_gate"].T)
        A2 = np.ascontiguousarray(
            inputs["L1_ffn_A_w"].transpose(2, 0, 1).reshape(D, ER))
        B2 = np.ascontiguousarray(
            inputs["L1_ffn_B_w"].transpose(0, 2, 1).reshape(ER, D))
    bw_r = modeT_slabs(bw, 512)
    rt_w = np.ascontiguousarray(
        rt.reshape(NDC, 128, NE).transpose(1, 0, 2).reshape(128, NDC * NE))
    a2_w = np.ascontiguousarray(
        A2.reshape(NDC, 128, ER).transpose(1, 0, 2).reshape(128, NDC * ER))
    return bw_r, rt_w, a2_w, B2


def kernel(**inputs):
    cores = list(range(8))
    x = inputs["L0_attn_x"]
    maps = []
    per_layer = {}
    for L in (0, 1):
        per_layer[L] = {
            "wq_r": modeF_slabs(inputs[f"L{L}_attn_Wq"], NDC),
            "wk_r": modeF_slabs(inputs[f"L{L}_attn_Wk"], KD // 128),
            "wv_r": modeT_slabs(inputs[f"L{L}_attn_Wv"], 512),
            "wo_r": modeT_slabs(inputs[f"L{L}_attn_Wo"], 512),
        }
        per_layer[L]["bw_r"], per_layer[L]["rt_w"], per_layer[L]["a2_w"], \
            per_layer[L]["b2_w"] = _ffn_host(inputs, L)
    for cc in cores:
        half = cc % 2
        pos = _pos_of(cc)
        m = {
            "xin": np.ascontiguousarray(x[cc // 2][pos]),
            "kvidx": (half * TPC + np.arange(8)[None, :] * 128
                      + np.arange(128)[:, None]).astype(np.int32),
            "fidx": np.array([[2 * half], [2 * half + 1]], dtype=np.int32),
            "epin": np.full((2, 16), MAGIC, dtype=np.int32),
        }
        for L in (0, 1):
            base = 500000.0 if L == 0 else 10000.0
            qc, qs_ = feat_tables(base, pos, ISQ)
            kc, ks_ = feat_tables(base, pos, 1.0)
            m[f"qcos{L}"], m[f"qsin{L}"] = qc, qs_
            m[f"kcos{L}"], m[f"ksin{L}"] = kc, ks_
            m[f"maskb{L}"] = _mask_packed(cc, L)
            for k2, v2 in per_layer[L].items():
                m[f"{k2}{L}"] = v2
        m["bias_w"] = inputs["L1_ffn_b_base"][None, :]
        m["thr_w"] = np.ascontiguousarray(inputs["L1_ffn_W_thr"].reshape(NDC, 128).T)
        m["bthr"] = inputs["L1_ffn_b_thr"].reshape(1, 1)
        maps.append(m)
    r = run_bass_kernel_spmd(_prog("full", build_full), maps, core_ids=cores)
    out = np.empty((B, T, D), np.float32)
    for cc in cores:
        out[cc // 2][_pos_of(cc)] = r.results[cc]["xout"]
    return out


def kernel_2launch(**inputs):
    cores = list(range(8))
    x = inputs["L0_attn_x"]
    state = [np.ascontiguousarray(x[c // 2, (c % 2) * TPC:(c % 2 + 1) * TPC])
             for c in cores]
    for L in (0, 1):
        base = 500000.0 if L == 0 else 10000.0
        wq_r = modeF_slabs(inputs[f"L{L}_attn_Wq"], NDC)
        wk_r = modeF_slabs(inputs[f"L{L}_attn_Wk"], KD // 128)
        wv_r = modeT_slabs(inputs[f"L{L}_attn_Wv"], 512)
        wo_r = modeT_slabs(inputs[f"L{L}_attn_Wo"], 512)
        bw_r, rt_w, a2_w, B2 = _ffn_host(inputs, L)
        maps = []
        for cc in cores:
            half = cc % 2
            pos = half * TPC + np.arange(TPC)
            qc, qs_ = feat_tables(base, pos, ISQ)
            kc, ks_ = feat_tables(base, pos, 1.0)
            m = {"xin": state[cc], "wq_r": wq_r, "wk_r": wk_r, "wv_r": wv_r,
                 "qcos": qc, "qsin": qs_, "kcos": kc, "ksin": ks_,
                 "wo_r": wo_r, "maskb": _mask_for(cc, L),
                 "kvidx": (half * TPC + np.arange(8)[None, :] * 128
                           + np.arange(128)[:, None]).astype(np.int32),
                 "fidx": np.array([[2 * half], [2 * half + 1]], dtype=np.int32),
                 "epin": np.full((2, 16), MAGIC + L, dtype=np.int32),
                 "bw_r": bw_r, "rt_w": rt_w, "a2_w": a2_w, "b2_w": B2}
            if L == 1:
                m["bias_w"] = inputs["L1_ffn_b_base"][None, :]
                m["thr_w"] = np.ascontiguousarray(
                    inputs["L1_ffn_W_thr"].reshape(NDC, 128).T)
                m["bthr"] = inputs["L1_ffn_b_thr"].reshape(1, 1)
            maps.append(m)
        r = run_bass_kernel_spmd(_prog(f"layer{L}", lambda L=L: build_layer(L)),
                                 maps, core_ids=cores)
        state = [np.ascontiguousarray(r.results[cc]["xout"]) for cc in cores]
    out = np.empty((B, T, D), np.float32)
    for cc in cores:
        out[cc // 2, (cc % 2) * TPC:(cc % 2 + 1) * TPC] = state[cc]
    return out


def kernel_unfused(**inputs):
    cores = list(range(8))
    x = inputs["L0_attn_x"]
    state = [np.ascontiguousarray(x[c // 2, (c % 2) * TPC:(c % 2 + 1) * TPC])
             for c in cores]

    for L in (0, 1):
        base = 500000.0 if L == 0 else 10000.0
        wq = inputs[f"L{L}_attn_Wq"]
        wk = inputs[f"L{L}_attn_Wk"]
        wv = inputs[f"L{L}_attn_Wv"]
        wo = inputs[f"L{L}_attn_Wo"]
        wq_r = modeF_slabs(wq, NDC)
        wk_r = modeF_slabs(wk, KD // 128)
        wv_r = modeT_slabs(wv, 512)
        wo_r = modeT_slabs(wo, 512)

        maps = []
        for c in cores:
            half = c % 2
            pos = half * TPC + np.arange(TPC)
            qc, qs_ = feat_tables(base, pos, ISQ)
            kc, ks_ = feat_tables(base, pos, 1.0)
            maps.append({"xin": state[c], "wq_r": wq_r, "wk_r": wk_r,
                         "wv_r": wv_r, "qcos": qc, "qsin": qs_,
                         "kcos": kc, "ksin": ks_})
        r1 = run_bass_kernel_spmd(_prog("qkv", build_qkv), maps, core_ids=cores)

        ktf, vtf = {}, {}
        for b in range(B):
            ktf[b] = np.ascontiguousarray(np.concatenate(
                [r1.results[2 * b]["kt"], r1.results[2 * b + 1]["kt"]], axis=1))
            vtf[b] = np.ascontiguousarray(np.concatenate(
                [r1.results[2 * b]["vt"], r1.results[2 * b + 1]["vt"]], axis=0))

        maps = []
        for c in cores:
            b = c // 2
            maps.append({"qt": r1.results[c]["qt"], "ktf": ktf[b],
                         "vtf": vtf[b], "maskb": _mask_for(c, L),
                         "xin": state[c], "wo_r": wo_r})
        r2 = run_bass_kernel_spmd(_prog("attn", build_attn), maps, core_ids=cores)
        state = [np.ascontiguousarray(r2.results[c]["xout"]) for c in cores]

        if L == 0:
            bw = np.ascontiguousarray(inputs["L0_ffn_base_W"].T)
            rt = np.ascontiguousarray(inputs["L0_ffn_router_W"].T)
            A2 = np.ascontiguousarray(
                inputs["L0_ffn_lora_A"].transpose(1, 0, 2).reshape(D, ER))
            B2 = np.ascontiguousarray(inputs["L0_ffn_lora_B"].reshape(ER, D))
        else:
            bw = np.ascontiguousarray(inputs["L1_ffn_W_base"].T)
            rt = np.ascontiguousarray(inputs["L1_ffn_W_gate"].T)
            A2 = np.ascontiguousarray(
                inputs["L1_ffn_A_w"].transpose(2, 0, 1).reshape(D, ER))
            B2 = np.ascontiguousarray(
                inputs["L1_ffn_B_w"].transpose(0, 2, 1).reshape(ER, D))
        bw_r = modeT_slabs(bw, 512)
        rt_w = np.ascontiguousarray(
            rt.reshape(NDC, 128, NE).transpose(1, 0, 2).reshape(128, NDC * NE))
        a2_w = np.ascontiguousarray(
            A2.reshape(NDC, 128, ER).transpose(1, 0, 2).reshape(128, NDC * ER))
        maps = []
        for c in cores:
            m = {"xin": state[c], "bw_r": bw_r, "rt_w": rt_w, "a2_w": a2_w,
                 "b2_w": B2}
            if L == 1:
                m["bias_w"] = inputs["L1_ffn_b_base"][None, :]
                m["thr_w"] = np.ascontiguousarray(
                    inputs["L1_ffn_W_thr"].reshape(NDC, 128).T)
                m["bthr"] = inputs["L1_ffn_b_thr"].reshape(1, 1)
            maps.append(m)
        r3 = run_bass_kernel_spmd(_prog(f"ffn{L}", lambda L=L: build_ffn(L)),
                                  maps, core_ids=cores)
        state = [np.ascontiguousarray(r3.results[c]["xout"]) for c in cores]

    out = np.empty((B, T, D), np.float32)
    for c in cores:
        out[c // 2, (c % 2) * TPC:(c % 2 + 1) * TPC] = state[c]
    return out
